# version 16
# baseline (speedup 1.0000x reference)
import numpy as np
import ml_dtypes
from contextlib import ExitStack

import concourse.bass as bass
import concourse.mybir as mybir
from concourse.bass import IndirectOffsetOnAxis
from concourse.bass_utils import run_bass_kernel_spmd

F32 = mybir.dt.float32
BF16 = mybir.dt.bfloat16
I32 = mybir.dt.int32
AF = mybir.ActivationFunctionType
ALU = mybir.AluOpType
AX = mybir.AxisListType
NPBF = ml_dtypes.bfloat16

B_TOT, T, D = 16, 2048, 1024
NCORES = 8
BPC = 2
NT = 16
NG = 4
DFF = 4096
NE = 8
FB = 512
NFB = DFF // FB
NTOK = BPC * T
CH = 512
NCHUNK = NTOK // CH + NE
NSLOT = NCHUNK * CH
EPS = 1e-5

ENGS = ("sync", "tensor", "vector", "scalar", "gpsimd")


class Sem:
    def __init__(self, handle, name):
        self.h = handle
        self.n = 0
        self.name = name


class Plan:
    def __init__(self):
        self.items = {e: [] for e in ENGS}
        self.ready = {}
        self.guard = {}

    def op(self, eng, fn, sem=None, inc=1):
        if sem is not None:
            sem.n += inc
            v = sem.n
            self.items[eng].append(lambda E, fn=fn, s=sem, k=inc: fn(E).then_inc(s.h, k))
            return v
        self.items[eng].append(fn)
        return None

    def wait(self, eng, sem, val):
        if val is None or val <= 0:
            return
        self.items[eng].append(lambda E, s=sem, v=val: E.wait_ge(s.h, v))

    def produced(self, key, sem, val):
        self.ready[key] = (sem, val)

    def waitr(self, eng, key):
        if key in self.ready:
            self.wait(eng, *self.ready[key])

    def consumed(self, key, sem, val):
        self.guard[key] = (sem, val)

    def need(self, eng, key):
        if key in self.guard:
            self.wait(eng, *self.guard[key])


def pack_host(inputs):
    f32 = np.float32
    W1 = np.asarray(inputs["L0_attn_W1"], f32)
    w2 = np.asarray(inputs["L0_attn_w2"], f32)
    Wc = np.asarray(inputs["L0_attn_Wc"], f32)
    b1 = np.asarray(inputs["L0_attn_b1"], f32)
    bc = np.asarray(inputs["L0_attn_bc"], f32)
    b2 = np.asarray(inputs["L0_attn_b2"], f32)
    rW = np.asarray(inputs["L0_ffn_router_W"], f32)
    rb = np.asarray(inputs["L0_ffn_router_b"], f32)
    fW1 = np.asarray(inputs["L0_ffn_W1"], f32)
    fb1 = np.asarray(inputs["L0_ffn_b1"], f32)
    fW2 = np.asarray(inputs["L0_ffn_W2"], f32)
    fb2 = np.asarray(inputs["L0_ffn_b2"], f32)

    d = {}
    d["w1t_p"] = np.ascontiguousarray(
        W1.T.reshape(8, 128, 1024).transpose(1, 0, 2).reshape(128, 8192))
    d["wct_p"] = np.ascontiguousarray(
        Wc.T.reshape(16, 128, 1024).transpose(1, 0, 2).reshape(128, 16384))
    d["w2c_p"] = np.ascontiguousarray(w2[0].reshape(8, 128).T)
    d["b1a_p"] = np.ascontiguousarray(b1.reshape(8, 128).T)
    d["bca_p"] = np.ascontiguousarray(bc.reshape(8, 128).T)
    d["scs_p"] = np.zeros((1, 8), f32)
    d["scs_p"][0, 0] = b2[0]
    d["rwcf_p"] = np.ascontiguousarray(
        rW.reshape(8, 128, 8).transpose(1, 0, 2).reshape(128, 64))
    d["rbc_p"] = np.ascontiguousarray(np.broadcast_to(rb[None, :], (128, 8)))
    w1p = fW1.reshape(8, 8, 128, 8, 512).transpose(0, 3, 2, 1, 4).reshape(8, 8, 128, 4096)
    w2p = fW2.reshape(8, 8, 4, 128, 1024).transpose(0, 1, 3, 2, 4).reshape(8, 8, 128, 4096)
    d["fw_p"] = np.ascontiguousarray(np.concatenate([w1p, w2p], axis=3))
    fbias = np.concatenate([
        fb1.reshape(8, 32, 128).transpose(0, 2, 1),
        fb2.reshape(8, 8, 128).transpose(0, 2, 1),
    ], axis=2)
    d["fbias_p"] = np.ascontiguousarray(fbias)
    idn = np.eye(128)
    d["idb_c"] = idn.astype(NPBF)
    d["idf_c"] = idn.astype(f32)
    U = np.triu(np.ones((128, 128)))
    d["Ub_c"] = U.astype(NPBF)
    d["Uf_c"] = U.astype(f32)
    d["onesb_c"] = np.ones((128, 128), NPBF)
    d["onesf_c"] = np.ones((128, 128), f32)
    d["iota8_c"] = np.broadcast_to(np.arange(8, dtype=f32)[None, :], (128, 8)).copy()
    d["iotac_c"] = (np.arange(128, dtype=f32) % 8)[:, None].copy()
    d["iotap_c"] = np.arange(128, dtype=f32)[:, None].copy()
    d["iota16_c"] = np.broadcast_to(np.arange(16, dtype=f32)[None, :], (8, 16)).copy()
    d["oob_c"] = np.full((NSLOT, 1), 1.0e9, f32)
    fbpat = np.zeros((128, 128), f32)
    for jj in range(16):
        for fb in range(8):
            fbpat[:, jj * 8 + fb] = fb * 128 + np.arange(128)
    d["fbpat_c"] = fbpat
    d["u32s_c"] = np.triu(np.ones((32, 32)), 1).astype(f32)
    d["tcol_c"] = np.broadcast_to(np.arange(32, dtype=f32)[None, :], (128, 32)).copy()
    tok = np.zeros((128, 32), f32)
    for t in range(32):
        tok[:, t] = (t // 16) * T + (t % 16) * 128 + np.arange(128)
    d["tokall_c"] = tok
    return d


def build(stage=0):
    nc = bass.Bass("TRN2", target_bir_lowering=False)
    P = Plan()

    x_h = nc.dram_tensor("x", [BPC, T, D], F32, kind="ExternalInput").ap()
    out_h = nc.dram_tensor("out", [BPC, T, D], F32, kind="ExternalOutput").ap()
    h_in = {}
    for name, shape in [
        ("w1t_p", [128, 8192]), ("wct_p", [128, 16384]), ("w2c_p", [128, 8]),
        ("b1a_p", [128, 8]), ("bca_p", [128, 8]), ("scs_p", [1, 8]),
        ("rwcf_p", [128, 64]), ("rbc_p", [128, 8]),
        ("fw_p", [NE, NFB, 128, 8192]), ("fbias_p", [NE, 128, 40]),
        ("idf_c", [128, 128]), ("Uf_c", [128, 128]),
        ("onesf_c", [128, 128]), ("iota8_c", [128, 8]), ("iotac_c", [128, 1]),
        ("iotap_c", [128, 1]), ("iota16_c", [8, 16]), ("oob_c", [NSLOT, 1]),
        ("fbpat_c", [128, 128]), ("u32s_c", [32, 32]),
        ("tcol_c", [128, 32]), ("tokall_c", [128, 32]),
    ]:
        h_in[name] = nc.dram_tensor(name, shape, F32, kind="ExternalInput").ap()
    for name, shape in [
        ("idb_c", [128, 128]), ("Ub_c", [128, 128]),
        ("onesb_c", [128, 128]),
    ]:
        h_in[name] = nc.dram_tensor(name, shape, BF16, kind="ExternalInput").ap()

    xn2_d = nc.dram_tensor("xn2_d", [NTOK, D], BF16, kind="Internal").ap()
    y1_d = nc.dram_tensor("y1_d", [NTOK, D], F32, kind="Internal").ap()
    inv_d = nc.dram_tensor("inv_d", [NSLOT, 1], F32, kind="Internal").ap()
    moe_d = nc.dram_tensor("moe_d", [NTOK, D], F32, kind="Internal").ap()
    baseflat = nc.dram_tensor("baseflat_d", [256, 1], F32, kind="Internal").ap()
    g_d = nc.dram_tensor("g_d", [NTOK, 1], F32, kind="Internal").ap()
    fw_flat = h_in["fw_p"].rearrange("e f p c -> (e f p) c")
    fbias_flat = h_in["fbias_p"].rearrange("e p c -> (e p) c")

    dbg = {}
    if stage:
        for nm, shp, dt in [("dbg_stream", [128, 16384], F32), ("dbg_xnT", [128, 16384], BF16),
                            ("dbg_rows", [128, 2048], F32), ("dbg_etm", [128, 16], F32),
                            ("dbg_g", [128, 32], F32), ("dbg_t1", [128, 32], BF16),
                            ("dbg_m8", [128, 8], F32), ("dbg_inv", [NSLOT, 1], F32),
                            ("dbg_xn2", [NTOK, D], BF16), ("dbg_moe", [NTOK, D], F32)]:
            dbg[nm] = nc.dram_tensor(nm, shp, dt, kind="ExternalOutput").ap()

    es = ExitStack()

    LC = Sem(es.enter_context(nc.semaphore("LC")), "LC")
    LX = Sem(es.enter_context(nc.semaphore("LX")), "LX")
    LWA = Sem(es.enter_context(nc.semaphore("LWA")), "LWA")
    LW = [Sem(es.enter_context(nc.semaphore(f"LW{i}")), f"LW{i}") for i in range(4)]
    GX = [Sem(es.enter_context(nc.semaphore("GX0")), "GX0"),
          Sem(es.enter_context(nc.semaphore("GX1")), "GX1")]
    GI = [Sem(es.enter_context(nc.semaphore("GI0")), "GI0"),
          Sem(es.enter_context(nc.semaphore("GI1")), "GI1")]
    GB = [Sem(es.enter_context(nc.semaphore("GB0")), "GB0"),
          Sem(es.enter_context(nc.semaphore("GB1")), "GB1")]
    GS = Sem(es.enter_context(nc.semaphore("GS")), "GS")
    GS2 = [Sem(es.enter_context(nc.semaphore(f"GS2{i}")), f"GS2{i}") for i in range(2)]
    C8S = Sem(es.enter_context(nc.semaphore("C8S")), "C8S")
    GD = Sem(es.enter_context(nc.semaphore("GD")), "GD")
    BFD = Sem(es.enter_context(nc.semaphore("BFD")), "BFD")
    BSG = Sem(es.enter_context(nc.semaphore("BSG")), "BSG")
    MT = [Sem(es.enter_context(nc.semaphore(f"MTm{i}")), f"MTm{i}") for i in range(4)]
    MT2 = [Sem(es.enter_context(nc.semaphore(f"MTy{i}")), f"MTy{i}") for i in range(4)]
    XT = [Sem(es.enter_context(nc.semaphore("XT0")), "XT0"),
          Sem(es.enter_context(nc.semaphore("XT1")), "XT1")]
    Y1S = Sem(es.enter_context(nc.semaphore("Y1S")), "Y1S")
    SO = [Sem(es.enter_context(nc.semaphore(f"SO{i}")), f"SO{i}") for i in range(4)]
    ST = Sem(es.enter_context(nc.semaphore("ST")), "ST")
    PE = Sem(es.enter_context(nc.semaphore("PE")), "PE")
    ACT = Sem(es.enter_context(nc.semaphore("ACT")), "ACT")
    DVE = Sem(es.enter_context(nc.semaphore("DVE")), "DVE")

    sb = lambda n, s, dt: es.enter_context(nc.sbuf_tensor(n, s, dt))
    rows_a = sb("rows_a", [128, T], F32)
    tmp32 = sb("tmp32", [128, 1024], F32)
    sqscr = sb("sqscr", [128, 1024], BF16)
    recbc = sb("recbc", [128, 128], F32)
    e_tm = sb("e_tm", [128, 16], F32)
    sums = sb("sums", [128, 16], F32)
    ssq = sb("ssq", [128, 16], F32)
    mcol = sb("mcol", [128, 16], F32)
    varc = sb("varc", [128, 16], F32)
    rstd = sb("rstd", [128, 16], F32)
    nbias = sb("nbias", [128, 16], F32)
    rstd3 = sb("rstd3", [128, 16], F32)
    nbias3 = sb("nbias3", [128, 16], F32)
    gcol = sb("gcol", [128, 32], F32)
    t1cols = sb("t1cols", [128, 32], BF16)
    t1f = sb("t1f", [128, 32], F32)
    lgscr = sb("lgscr", [128, 64], F32)
    ncarry = sb("ncarry", [128, 8], F32)
    scs = sb("scs", [1, 8], F32)
    xn2fT = sb("xn2fT", [128, 1024], F32)
    idb = sb("idb", [128, 128], BF16)
    idf = sb("idf", [128, 128], F32)
    Ub = sb("Ub", [128, 128], BF16)
    Uf = sb("Uf", [128, 128], F32)
    onesb = sb("onesb", [128, 128], BF16)
    onesf = sb("onesf", [128, 128], F32)
    iota8 = sb("iota8", [128, 8], F32)
    iotac = sb("iotac", [128, 1], F32)
    iotap = sb("iotap", [128, 1], F32)
    iota16 = sb("iota16", [8, 16], F32)
    rbc = sb("rbc", [128, 8], F32)
    w2c = sb("w2c", [128, 8], BF16)
    b1a = sb("b1a", [128, 8], F32)
    bca = sb("bca", [128, 8], F32)
    rwcf = sb("rwcf", [128, 64], F32)
    fbpat = sb("fbpat", [128, 128], F32)
    m8 = sb("m8", [128, 8], F32)
    echunk_bc = sb("echunk_bc", [128, 16], F32)
    oh_all = sb("oh_all", [128, 256], F32)
    c8T = sb("c8T", [32, 8], F32)
    cntsb = sb("cntsb", [128, 32], F32)
    rkm = sb("rkm", [128, 32], F32)
    rank_all = sb("rank_all", [128, 32], F32)
    bsel_all = sb("bsel_all", [128, 32], F32)
    base32 = sb("base32", [8, 32], F32)
    idxgf = sb("idxgf", [128, 32], F32)
    idxgi = sb("idxgi", [128, 32], I32)
    dest32f = sb("dest32f", [128, 32], F32)
    dest32i = sb("dest32i", [128, 32], I32)
    tcol = sb("tcol", [128, 32], F32)
    tokall = sb("tokall", [128, 32], F32)
    u32s = sb("u32s", [32, 32], F32)
    gbuf = sb("gbuf", [128, 8], F32)

    pst = lambda n, s, dt: es.enter_context(nc.psum_tensor(n, s, dt))
    ps_tr = pst("ps_tr", [128, 1024], BF16)
    ps_h = [pst("ps_h0", [128, 512], F32), pst("ps_h1", [128, 512], F32)]
    ps_s = pst("ps_s", [128, 512], F32)
    ps_num = pst("ps_num", [128, 1024], F32)
    ps_wc = [pst("ps_wc0", [128, 512], F32), pst("ps_wc1", [128, 512], F32)]

    def dma(queue_eng, out_ap, in_ap, sem):
        return P.op(queue_eng, lambda E, o=out_ap, i=in_ap: E.dma_start(out=o, in_=i), sem, 16)

    _bc_regs = {}

    def idma(out_ap, out_off, in_ap, in_off, bc, sem):
        def fn(E, o=out_ap, oo=out_off, i=in_ap, io=in_off, b=bc):
            if b not in _bc_regs:
                _bc_regs[b] = E.to_reg(b)
            return E.indirect_dma_start(o, oo, i, io, bounds_check=_bc_regs[b],
                                        oob_is_err=False)
        return P.op("gpsimd", fn, sem, 16)

    def mm(out_ap, lh, rh, start, stop, sem=None):
        return P.op("tensor",
                    lambda t, o=out_ap, a=lh, b=rh, s1=start, s2=stop:
                    t.matmul(o, a, b, start=s1, stop=s2), sem, 1)

    def tr(out_ap, in_ap, fp32=False, sem=None):
        ident = (idf if fp32 else idb)[:]
        return P.op("tensor",
                    lambda t, o=out_ap, i=in_ap, d=ident: t.transpose(o, i, d), sem, 1)

    def act(out_ap, in_ap, f, bias=0.0, scale=1.0, accum=None, sem=None):
        P.wait("scalar", ACT, ACT.n)
        return P.op("scalar",
                    lambda s, o=out_ap, i=in_ap, ff=f, bb=bias, sc=scale, au=accum:
                    s.activation(o, i, ff, bias=bb, scale=sc, accum_out=au), ACT, 1)

    def dve(method, *aps, sem=None, **kw):
        def fn(v, m=method, a=aps, kw2=kw):
            return getattr(v, m)(*a, **kw2)
        P.wait("vector", DVE, DVE.n)
        return P.op("vector", fn, DVE, 1)

    for name, dst in [("idf_c", idf), ("Uf_c", Uf), ("onesf_c", onesf),
                      ("iota8_c", iota8), ("iotac_c", iotac), ("iotap_c", iotap),
                      ("iota16_c", iota16), ("rwcf_p", rwcf), ("fbpat_c", fbpat),
                      ("rbc_p", rbc), ("b1a_p", b1a),
                      ("bca_p", bca), ("scs_p", scs),
                      ("idb_c", idb), ("Ub_c", Ub), ("onesb_c", onesb),
                      ("u32s_c", u32s), ("tcol_c", tcol), ("tokall_c", tokall)]:
        dma("sync", dst[:], h_in[name], LC)
    dma("sync", inv_d, h_in["oob_c"], LC)
    v_consts = LC.n
    dma("gpsimd", w2c[:], h_in["w2c_p"], LWA)
    v_smallw = LWA.n
    for eng in ENGS:
        P.wait(eng, LC, v_consts)
        P.wait(eng, LWA, v_smallw)

    class StageStop(Exception):
        pass

    def stagecut(k, tap_pairs):
        if stage == k:
            for s_ in (PE, ACT, DVE, LX, LWA, LW[0], LW[1], GS, LC,
                       XT[0], XT[1], Y1S, C8S, BFD, BSG):
                P.wait("sync", s_, s_.n)
            for nm, ap in tap_pairs:
                dma("sync", dbg[nm], ap, ST)
            raise StageStop()

    def ln_stats_tile(src_ap, col, ready_key):
        P.waitr("vector", ready_key)
        dve("tensor_reduce", sums[:, col:col + 1], src_ap, sem=None,
            axis=AX.X, op=ALU.add)
        P.waitr("scalar", ready_key)
        act(sqscr[:, 0:src_ap.shape[1]], src_ap, AF.Square,
            accum=ssq[:, col:col + 1], sem=ACT)

    def ln_rstd(cols, n_elem, tag, out_r=None, out_b=None, ocols=None):
        a, b = cols.start, cols.stop
        oa, ob = (a, b) if ocols is None else (ocols.start, ocols.stop)
        out_r = rstd if out_r is None else out_r
        out_b = nbias if out_b is None else out_b
        w = b - a
        P.wait("vector", ACT, ACT.n)
        dve("tensor_scalar_mul", mcol[:, a:b], sums[:, a:b], 1.0 / n_elem)
        dve("tensor_scalar_mul", varc[:, a:b], ssq[:, a:b], 1.0 / n_elem)
        dve("tensor_tensor", lgscr[:, 40:40 + w], mcol[:, a:b], mcol[:, a:b], ALU.mult)
        dve("tensor_tensor", varc[:, a:b], varc[:, a:b], lgscr[:, 40:40 + w],
            ALU.subtract)
        v = dve("tensor_scalar_add", varc[:, a:b], varc[:, a:b], EPS, sem=DVE)
        P.wait("scalar", DVE, v)
        va = act(lgscr[:, 48:48 + w], varc[:, a:b], AF.Sqrt, sem=ACT)
        P.wait("vector", ACT, va)
        dve("reciprocal", out_r[:, oa:ob], lgscr[:, 48:48 + w])
        dve("tensor_tensor", out_b[:, oa:ob], mcol[:, a:b], out_r[:, oa:ob], ALU.mult)
        v = dve("tensor_scalar_mul", out_b[:, oa:ob], out_b[:, oa:ob], -1.0, sem=DVE)
        P.produced(("rstd", tag), DVE, v)

    def phase_a(b, AB, SA):
        xn_tmp, exnfm, exntm, hsb, afm, atm, ebc, ctxT = (
            AB["xn_tmp"], AB["exnfm"], AB["exntm"], AB["hsb"], AB["afm"],
            AB["atm"], AB["ebc"], AB["ctxT"])
        stream, xnT, w1t, wct = SA["stream"], SA["xnT"], SA["w1t"], SA["wct"]

        if b > 0:
            P.wait("sync", Y1S, P.ready[("y1stored", b - 1)][1])
        for i in range(NT):
            v = dma("sync", stream[:, i * 1024:(i + 1) * 1024],
                    x_h[b, i * 128:(i + 1) * 128, :], LX)
        for i in range(NT):
            P.produced(("x", b, i), LX, v)

        for i in range(NT):
            ln_stats_tile(stream[:, i * 1024:(i + 1) * 1024], i, ("x", b, i))
        ln_rstd(slice(0, NT), D, "ln1")

        for i in range(NT):
            buf = i % 2
            xt = xn_tmp[:, buf * 1024:(buf + 1) * 1024]
            P.waitr("scalar", ("rstd", "ln1"))
            P.need("scalar", ("xn_tmp", buf))
            va = act(xt, stream[:, i * 1024:(i + 1) * 1024], AF.Identity,
                     bias=nbias[:, i:i + 1], scale=rstd[:, i:i + 1], sem=ACT)
            P.wait("tensor", ACT, va)
            P.need("tensor", "ps_tr")
            for c in range(8):
                vt = tr(ps_tr[:, c * 128:(c + 1) * 128], xt[:, c * 128:(c + 1) * 128],
                        sem=PE if c == 7 else None)
            P.consumed(("xn_tmp", buf), PE, vt)
            P.wait("vector", PE, vt)
            dst = xnT.ap().rearrange("p (c t) -> p c t", c=8)[:, :, i * 128:(i + 1) * 128]
            src = ps_tr.ap().rearrange("p (c t) -> p c t", c=8)
            ve = dve("tensor_copy", dst, src, sem=DVE)
            P.consumed("ps_tr", DVE, ve)
            P.produced(("xnT", b, i), DVE, ve)

        stagecut(1, [("dbg_stream", stream[:]), ("dbg_xnT", xnT[:])])

        for g in range(NG):
            for i in range(4 * g, 4 * g + 4):
                P.waitr("tensor", ("xnT", b, i))
            P.waitr("tensor", "w1t")
            va_tanh = [None] * 8
            vs = None
            for m in range(9):
                if m < 8:
                    pb = m % 2
                    P.need("tensor", ("ps_h", pb))
                    for c in range(8):
                        vh = mm(ps_h[pb][:],
                                w1t[:, c * 1024 + m * 128: c * 1024 + (m + 1) * 128],
                                xnT[:, c * T + g * 512: c * T + (g + 1) * 512],
                                start=(c == 0), stop=(c == 7),
                                sem=PE if c == 7 else None)
                    P.wait("scalar", PE, vh)
                    va_tanh[m] = act(hsb[:, pb * 512:(pb + 1) * 512], ps_h[pb][:],
                                     AF.Tanh, bias=b1a[:, m:m + 1], sem=ACT)
                    P.consumed(("ps_h", pb), ACT, va_tanh[m])
                if m >= 1:
                    mp = m - 1
                    P.wait("tensor", ACT, va_tanh[mp])
                    if mp == 0:
                        P.need("tensor", "ps_s")
                    vs = mm(ps_s[0:1, :], w2c[:, mp:mp + 1],
                            hsb[:, (mp % 2) * 512:((mp % 2) + 1) * 512],
                            start=(mp == 0), stop=(mp == 7), sem=PE)
                    P.consumed(("hsb", mp % 2), PE, vs)
            P.wait("vector", PE, vs)
            ve = dve("tensor_copy", rows_a[0:1, g * 512:(g + 1) * 512], ps_s[0:1, :], sem=DVE)
            P.consumed("ps_s", DVE, ve)

        dve("tensor_reduce", scs[0:1, 1:2], rows_a[0:1, :], axis=AX.X, op=ALU.max)
        v = dve("tensor_tensor", scs[0:1, 2:3], scs[0:1, 0:1], scs[0:1, 1:2],
                ALU.subtract, sem=DVE)
        P.wait("scalar", DVE, v)
        vex = act(rows_a[32:33, :], rows_a[0:1, :], AF.Exp, bias=scs[0:1, 2:3], sem=ACT)
        P.produced("e_row", ACT, vex)

        P.waitr("tensor", "e_row")
        P.need("tensor", "ps_s")
        for i in range(NT):
            vt = mm(ps_s[:, i:i + 1], rows_a[32:33, i * 128:(i + 1) * 128],
                    onesf[32:33, 0:1], True, True, sem=PE if i == NT - 1 else None)
        P.wait("vector", PE, vt)
        ve = dve("tensor_copy", e_tm[:], ps_s[:, 0:16], sem=DVE)
        P.consumed("ps_s", DVE, ve)
        P.produced("e_tm", DVE, ve)
        dve("memset", ncarry[:], 0.0)
        dve("memset", scs[0:1, 3:4], 0.0)

        stagecut(2, [("dbg_rows", rows_a[:]), ("dbg_etm", e_tm[:]),
                     ("dbg_xnT", xnT[:])])

        def cumsum_group(g):
            pb = g % 2
            ctxg = ctxT[pb]
            P.waitr("tensor", "e_row")
            P.need("tensor", "ps_num")
            vbq = mm(ps_num[:, 0:512], onesf[32:33, :],
                     rows_a[32:33, g * 512:(g + 1) * 512], True, True, sem=PE)
            P.wait("vector", PE, vbq)
            veb = dve("tensor_copy", ebc[:], ps_num[:, 0:512], sem=DVE)
            P.consumed("ps_num", DVE, veb)
            vc = None
            for k in range(4):
                i = 4 * g + k
                P.need("vector", "exnfm")
                for c in range(8):
                    vx = dve("tensor_tensor", exnfm[:, c * 128:(c + 1) * 128],
                             xnT[:, c * T + i * 128: c * T + (i + 1) * 128],
                             ebc[:, k * 128:(k + 1) * 128], ALU.mult,
                             sem=DVE if c == 7 else None)
                P.produced("exnfm", DVE, vx)
                P.waitr("tensor", "exnfm")
                P.need("tensor", "ps_tr")
                for c in range(8):
                    vt = tr(ps_tr[:, c * 128:(c + 1) * 128], exnfm[:, c * 128:(c + 1) * 128],
                            sem=PE if c == 7 else None)
                P.consumed("exnfm", PE, vt)
                P.wait("vector", PE, vt)
                P.need("vector", "exntm")
                ve = dve("tensor_copy", exntm[:], ps_tr[:, 0:1024], sem=DVE)
                P.consumed("ps_tr", DVE, ve)
                P.produced("exntm", DVE, ve)
                P.waitr("tensor", "e_tm")
                P.need("tensor", "ps_s")
                vd = mm(ps_s[0:1, 0:128], e_tm[:, i:i + 1], Uf[:], True, True, sem=PE)
                P.wait("vector", PE, vd)
                dve("tensor_scalar_add", rows_a[0:1, i * 128:(i + 1) * 128],
                    ps_s[0:1, 0:128], scs[0:1, 3:4])
                dve("tensor_copy", scs[0:1, 3:4], rows_a[0:1, i * 128 + 127: i * 128 + 128])
                vr = dve("reciprocal", rows_a[64:65, i * 128:(i + 1) * 128],
                         rows_a[0:1, i * 128:(i + 1) * 128], sem=DVE)
                P.consumed("ps_s", DVE, vr)
                P.wait("tensor", DVE, vr)
                P.need("tensor", "ps_s")
                vb = mm(ps_s[:, 128:256], onesf[64:65, :],
                        rows_a[64:65, i * 128:(i + 1) * 128], True, True, sem=PE)
                P.wait("vector", PE, vb)
                P.need("vector", "recbc")
                vrb = dve("tensor_copy", recbc[:], ps_s[:, 128:256], sem=DVE)
                P.consumed("ps_s", DVE, vrb)
                P.produced("recbc", DVE, vrb)
                P.waitr("tensor", "exntm")
                P.need("tensor", "ps_num")
                for c in range(8):
                    vn = mm(ps_num[:, c * 128:(c + 1) * 128], exntm[:, c * 128:(c + 1) * 128],
                            Ub[:], True, True, sem=PE if c == 7 else None)
                P.consumed("exntm", PE, vn)
                P.wait("vector", PE, vn)
                P.waitr("vector", "recbc")
                if k == 0:
                    P.need("vector", ("ctxT", pb))
                for c in range(8):
                    dve("scalar_tensor_tensor",
                        ctxg[:, c * 512 + k * 128: c * 512 + (k + 1) * 128],
                        ps_num[:, c * 128:(c + 1) * 128], ncarry[:, c:c + 1], recbc[:],
                        ALU.add, ALU.mult)
                    vc = dve("tensor_tensor", ncarry[:, c:c + 1],
                             ncarry[:, c:c + 1], ps_num[:, c * 128 + 127: c * 128 + 128],
                             ALU.add, sem=DVE if c == 7 else None)
                P.consumed("ps_num", DVE, vc)
                P.consumed("recbc", DVE, vc)
            P.produced(("ctx", g), DVE, vc)

        def tail_group(g):
            pb = g % 2
            ctxg = ctxT[pb]
            P.waitr("tensor", ("ctx", g))
            P.waitr("tensor", "wct")
            vw_last = None
            for mp in range(4):
                va_pair = None
                for mmi in range(2):
                    m = 2 * mp + mmi
                    P.need("tensor", ("ps_wc", mmi))
                    for kk in range(16):
                        if kk < 8:
                            rh = ctxg[:, kk * 512:(kk + 1) * 512]
                        else:
                            rh = xnT[:, (kk - 8) * T + g * 512:(kk - 8) * T + (g + 1) * 512]
                        vw = mm(ps_wc[mmi][:],
                                wct[:, kk * 1024 + m * 128: kk * 1024 + (m + 1) * 128],
                                rh, start=(kk == 0), stop=(kk == 15),
                                sem=PE if kk == 15 else None)
                    vw_last = vw
                    P.wait("scalar", PE, vw)
                    P.need("scalar", ("afm", 0))
                    va_pair = act(afm[:, mmi * 512:(mmi + 1) * 512],
                                  ps_wc[mmi][:], AF.Tanh, bias=bca[:, m:m + 1], sem=ACT)
                    P.consumed(("ps_wc", mmi), ACT, va_pair)
                P.wait("tensor", ACT, va_pair)
                P.need("tensor", "ps_tr")
                for k in range(4):
                    for mmi in range(2):
                        vt = tr(ps_tr[:, (2 * k + mmi) * 128:(2 * k + mmi + 1) * 128],
                                afm[:, mmi * 512 + k * 128: mmi * 512 + (k + 1) * 128],
                                sem=PE if (k == 3 and mmi == 1) else None)
                P.consumed(("afm", 0), PE, vt)
                P.wait("vector", PE, vt)
                dst = atm.ap().rearrange("p (k q f) -> p k q f", k=4, q=8)[:, :, 2 * mp:2 * mp + 2, :]
                src = ps_tr.ap().rearrange("p (k mmi f) -> p k mmi f", k=4, mmi=2)
                ve = dve("tensor_copy", dst, src, sem=DVE)
                P.consumed("ps_tr", DVE, ve)
            P.produced(("atm", g), DVE, ve)
            P.consumed(("ctxT", pb), PE, vw_last)

            for k in range(4):
                ln_stats_tile(atm[:, k * 1024:(k + 1) * 1024], k, ("atm", g))
            ln_rstd(slice(0, 4), D, "ln2")
            for k in range(4):
                i = 4 * g + k
                P.waitr("scalar", ("rstd", "ln2"))
                P.need("scalar", "tmp32")
                va = act(tmp32[:], atm[:, k * 1024:(k + 1) * 1024], AF.Identity,
                         bias=nbias[:, k:k + 1], scale=rstd[:, k:k + 1], sem=ACT)
                P.wait("vector", ACT, va)
                vy = dve("tensor_tensor", stream[:, i * 1024:(i + 1) * 1024],
                         stream[:, i * 1024:(i + 1) * 1024], tmp32[:], ALU.add, sem=DVE)
                P.consumed("tmp32", DVE, vy)
                P.produced(("y1", b, i), DVE, vy)
            for k in range(4):
                i = 4 * g + k
                ln_stats_tile(stream[:, i * 1024:(i + 1) * 1024], 4 + k, ("y1", b, i))
            ln_rstd(slice(4, 8), D, "ln3", out_r=rstd3, out_b=nbias3,
                    ocols=slice(4 * g, 4 * g + 4))
            for k in range(4):
                i = 4 * g + k
                buf = k % 2
                xt = xn_tmp[:, buf * 1024:(buf + 1) * 1024]
                P.waitr("scalar", ("rstd", "ln3"))
                P.need("scalar", ("xn_tmp", buf))
                va = act(xt, stream[:, i * 1024:(i + 1) * 1024], AF.Identity,
                         bias=nbias3[:, i:i + 1], scale=rstd3[:, i:i + 1], sem=ACT)
                P.wait("sync", ACT, va)
                vs2 = dma("sync", xn2_d[(b * T + i * 128):(b * T + (i + 1) * 128), :], xt,
                          XT[buf])
                P.consumed(("xn_tmp", buf), XT[buf], vs2)

            vg = vs1 = None
            for i in range(4 * g, 4 * g + 4):
                P.waitr("scalar", ("rstd", "ln3"))
                P.need("scalar", "tmp32")
                vaf = act(tmp32[:], stream[:, i * 1024:(i + 1) * 1024], AF.Identity,
                          bias=nbias3[:, i:i + 1], scale=rstd3[:, i:i + 1], sem=ACT)
                P.wait("tensor", ACT, vaf)
                P.need("tensor", "ps_num")
                for c in range(8):
                    vt = tr(ps_num[:, c * 128:(c + 1) * 128], tmp32[:, c * 128:(c + 1) * 128],
                            fp32=True, sem=PE if c == 7 else None)
                P.consumed("tmp32", PE, vt)
                P.wait("vector", PE, vt)
                P.need("vector", "xn2fT")
                vef = dve("tensor_copy", xn2fT[:], ps_num[:, 0:1024], sem=DVE)
                P.consumed("ps_num", DVE, vef)
                P.produced("xn2fT", DVE, vef)
                P.waitr("tensor", "xn2fT")
                P.need("tensor", "ps_s")
                for c in range(8):
                    vl = mm(ps_s[:, 256:264],
                            xn2fT[:, c * 128:(c + 1) * 128],
                            rwcf[:, c * 8:(c + 1) * 8], start=(c == 0), stop=(c == 7),
                            sem=PE if c == 7 else None)
                P.consumed("xn2fT", PE, vl)
                P.wait("vector", PE, vl)
                lg = lgscr[:, 0:8]
                dve("tensor_tensor", lg, ps_s[:, 256:264], rbc[:], ALU.add)
                vq = dve("tensor_reduce", lgscr[:, 8:9], lg, sem=DVE, axis=AX.X, op=ALU.max)
                P.consumed("ps_s", DVE, vq)
                dve("tensor_single_scalar", lgscr[:, 8:16], lg, lgscr[:, 8:9], ALU.is_equal)
                dve("tensor_tensor", lgscr[:, 16:24], lgscr[:, 8:16], iota8[:], ALU.mult)
                dve("tensor_scalar", lgscr[:, 24:32], lgscr[:, 8:16], -1000.0, 1000.0,
                    ALU.mult, ALU.add)
                dve("tensor_tensor", lgscr[:, 16:24], lgscr[:, 16:24], lgscr[:, 24:32], ALU.add)
                col = 16 * b + i
                dve("tensor_reduce", t1f[:, col:col + 1], lgscr[:, 16:24], axis=AX.X, op=ALU.min)
                dve("tensor_copy", t1cols[:, col:col + 1], t1f[:, col:col + 1])
                dve("tensor_reduce", lgscr[:, 24:25], lg, axis=AX.X, op=ALU.max)
                vneg = dve("tensor_scalar_mul", lgscr[:, 24:25], lgscr[:, 24:25], -1.0, sem=DVE)
                P.wait("scalar", DVE, vneg)
                va = act(lgscr[:, 32:40], lg, AF.Exp, bias=lgscr[:, 24:25], sem=ACT)
                P.wait("vector", ACT, va)
                dve("tensor_reduce", lgscr[:, 25:26], lgscr[:, 32:40], axis=AX.X, op=ALU.add)
                vg = dve("reciprocal", gcol[:, col:col + 1], lgscr[:, 25:26], sem=DVE)
                P.waitr("sync", ("y1", b, i))
                vs1 = dma("sync", y1_d[(b * T + i * 128):(b * T + (i + 1) * 128), :],
                          stream[:, i * 1024:(i + 1) * 1024], Y1S)
            return vg, vs1

        for g in range(NG):
            cumsum_group(g)
            if g >= 1:
                vg, vs1 = tail_group(g - 1)
        vg, vs1 = tail_group(NG - 1)
        P.produced("gates", DVE, vg)
        P.produced(("y1stored", b), Y1S, vs1)
        P.wait("sync", DVE, vg)
        for i in range(NT):
            dma("sync", g_d[(b * T + i * 128):(b * T + (i + 1) * 128), :],
                gcol[:, 16 * b + i:16 * b + i + 1], GD)

    def phase_ffn(SB):
        wbuf, y_sb, xg_tm, xgT, z_sb, bias_sb, idxf, idxi, idxrow, y1t = (
            SB["wbuf"], SB["y_sb"], SB["xg_tm"], SB["xgT"], SB["z"],
            SB["bias"], SB["idxf"], SB["idxi"], SB["idxrow"], SB["y1t"])
        ysc = [SB["ysc"][0], SB["ysc"][1], SB["moet"][0], SB["moet"][1]]

        v_xt = (XT[0].n, XT[1].n, Y1S.n)
        P.wait("gpsimd", XT[0], v_xt[0])
        P.wait("gpsimd", XT[1], v_xt[1])
        P.wait("gpsimd", Y1S, v_xt[2])
        P.wait("gpsimd", GD, GD.n)

        voh = None
        for t in range(2 * NT):
            voh = dve("tensor_single_scalar", oh_all[:, t * 8:(t + 1) * 8], iota8[:],
                      t1f[:, t:t + 1], ALU.is_equal,
                      sem=DVE if t == 2 * NT - 1 else None)
        vrank = None
        for g in range(8):
            hp = g % 2
            P.need("tensor", ("ps_h", hp))
            P.wait("tensor", DVE, voh)
            vm = None
            for q in range(4):
                t = 4 * g + q
                vm = mm(ps_h[hp][:, q * 8:(q + 1) * 8], Uf[:],
                        oh_all[:, t * 8:(t + 1) * 8], True, True,
                        sem=PE if q == 3 else None)
            P.wait("vector", PE, vm)
            P.need("vector", "cntsb")
            vq = dve("tensor_copy", cntsb[:], ps_h[hp][:, 0:32], sem=DVE)
            P.consumed(("ps_h", hp), DVE, vq)
            P.wait("sync", DVE, vq)
            vd = None
            for q in range(4):
                t = 4 * g + q
                vd = dma("sync", c8T[t:t + 1, :], cntsb[127:128, q * 8:(q + 1) * 8],
                         C8S)
            P.consumed("cntsb", C8S, vd)
            vmu = dve("tensor_tensor", rkm[:], cntsb[:],
                      oh_all[:, g * 32:(g + 1) * 32], ALU.mult, sem=DVE)
            P.wait("scalar", DVE, vmu)
            for q in range(4):
                t = 4 * g + q
                vrank = act(sqscr[:, 0:8], rkm[:, q * 8:(q + 1) * 8], AF.Identity,
                            accum=rank_all[:, t:t + 1], sem=ACT)
        P.wait("tensor", C8S, C8S.n)
        P.need("tensor", "ps_s")
        vm = mm(ps_s[0:8, 0:1], c8T[:], onesf[0:32, 0:1], True, True, sem=PE)
        P.wait("vector", PE, vm)
        vq = dve("tensor_copy", m8[0:8, 0:1], ps_s[0:8, 0:1], sem=DVE)
        P.consumed("ps_s", DVE, vq)
        dve("tensor_scalar", m8[0:8, 1:2], m8[0:8, 0:1], float(CH - 1), 1.0 / CH,
            ALU.add, ALU.mult)
        dve("tensor_copy", SB["capi"][0:8, 0:1], m8[0:8, 1:2])
        vq = dve("tensor_copy", m8[0:8, 1:2], SB["capi"][0:8, 0:1], sem=DVE)
        P.wait("tensor", DVE, vq)
        P.need("tensor", "ps_s")
        vm = mm(ps_s[0:1, 0:8], m8[0:8, 1:2], Uf[0:8, 0:8], True, True, sem=PE)
        P.wait("vector", PE, vm)
        vq = dve("tensor_copy", rows_a[0:1, 0:8], ps_s[0:1, 0:8], sem=DVE)
        P.consumed("ps_s", DVE, vq)
        P.wait("tensor", DVE, vq)
        P.need("tensor", "ps_s")
        vm = mm(ps_s[0:8, 8:9], rows_a[0:1, 0:8], onesf[0:1, 0:1], True, True, sem=PE)
        P.wait("vector", PE, vm)
        vq = dve("tensor_copy", m8[0:8, 2:3], ps_s[0:8, 8:9], sem=DVE)
        P.consumed("ps_s", DVE, vq)
        dve("tensor_tensor", m8[0:8, 3:4], m8[0:8, 2:3], m8[0:8, 1:2], ALU.subtract)
        dve("tensor_scalar_mul", m8[0:8, 3:4], m8[0:8, 3:4], float(CH))
        P.need("tensor", "ps_s")
        vm = mm(ps_s[0:8, 32:64], c8T[:], u32s[:], True, True, sem=PE)
        P.wait("vector", PE, vm)
        vb = dve("tensor_scalar", base32[:], ps_s[0:8, 32:64], m8[0:8, 3:4], -1.0,
                 ALU.add, ALU.add, sem=DVE)
        P.consumed("ps_s", DVE, vb)
        P.wait("sync", DVE, vb)
        vbf = dma("sync", baseflat.rearrange("(e t) one -> e (t one)", e=8),
                  base32[:], BFD)
        dve("scalar_tensor_tensor", idxgf[:], t1f[:], 32.0, tcol[:],
            ALU.mult, ALU.add)
        vi = dve("tensor_copy", idxgi[:], idxgf[:], sem=DVE)
        P.wait("gpsimd", DVE, vi)
        P.wait("gpsimd", BFD, vbf)
        for t in range(2 * NT):
            idma(bsel_all[:, t:t + 1], None, baseflat,
                 IndirectOffsetOnAxis(idxgi[:, t:t + 1], 0), 255, BSG)
        vq = dve("tensor_single_scalar", SB["cmp16"][0:8, :], iota16[:], m8[0:8, 2:3],
                 ALU.is_ge, sem=DVE)
        P.wait("tensor", DVE, vq)
        P.need("tensor", "ps_s")
        vm = mm(ps_s[0:1, 16:32], onesf[0:8, 0:1], SB["cmp16"][0:8, :], True, True, sem=PE)
        P.wait("vector", PE, vm)
        vq = dve("tensor_single_scalar", rows_a[0:1, 16:32], ps_s[0:1, 16:32], 7.0,
                 ALU.min, sem=DVE)
        P.consumed("ps_s", DVE, vq)
        P.wait("tensor", DVE, vq)
        P.need("tensor", "ps_s")
        vm = mm(ps_s[:, 32:48], onesf[0:1, :], rows_a[0:1, 16:32], True, True, sem=PE)
        P.wait("vector", PE, vm)
        vq = dve("tensor_copy", echunk_bc[:], ps_s[:, 32:48], sem=DVE)
        P.consumed("ps_s", DVE, vq)
        P.produced("echunk", DVE, vq)


        P.waitr("vector", "echunk")
        dve("tensor_scalar_mul", lgscr[:, 0:16], echunk_bc[:], 1024.0)
        for j in range(NCHUNK):
            dve("tensor_scalar_add", SB["widxf"][:, j * 8:(j + 1) * 8],
                fbpat[:, j * 8:(j + 1) * 8], lgscr[:, j:j + 1])
        vq = dve("tensor_copy", SB["widxi"][:], SB["widxf"][:], sem=DVE)
        P.produced("widxall", DVE, vq)
        for j in range(NCHUNK):
            dve("scalar_tensor_tensor", SB["bidxf"][:, j:j + 1], echunk_bc[:, j:j + 1],
                128.0, iotap[:], ALU.mult, ALU.add)
        vq = dve("tensor_copy", SB["bidxi"][:], SB["bidxf"][:], sem=DVE)
        P.produced("bidxall", DVE, vq)

        P.wait("vector", BSG, BSG.n)
        P.wait("vector", ACT, vrank)
        dve("tensor_tensor", dest32f[:], bsel_all[:], rank_all[:], ALU.add)
        vq = dve("tensor_copy", dest32i[:], dest32f[:], sem=DVE)
        P.wait("gpsimd", DVE, vq)
        vsc = None
        for t in range(2 * NT):
            vsc = idma(inv_d, IndirectOffsetOnAxis(dest32i[:, t:t + 1], 0),
                       tokall[:, t:t + 1], None, NSLOT - 1, GS)
        P.produced("invscat", GS, vsc)

        stagecut(5, [("dbg_g", gcol[:]), ("dbg_t1", t1cols[:]),
                     ("dbg_inv", inv_d), ("dbg_m8", m8[:]),
                     ("dbg_xn2", xn2_d)])

        inv_row = inv_d.rearrange("s one -> one s")
        out_flat = out_h.rearrange("b t d -> (b t) d")
        GG = GS2
        GY = MT
        OS = [SO, MT2]

        def load_chunk_indices(j, par):
            P.need("gpsimd", ("idxrow", par))
            P.waitr("gpsimd", "invscat")
            vr = dma("gpsimd", idxrow[par][:], inv_row[0:1, j * CH:(j + 1) * CH], GI[par])
            P.produced(("idxrowld", par), GI[par], vr)
            P.waitr("tensor", ("idxrowld", par))
            P.need("tensor", "ps_s")
            for s in range(4):
                vm = mm(ps_s[:, 448 + s:449 + s], idxrow[par][0:1, s * 128:(s + 1) * 128],
                        onesf[0:1, 0:1], True, True, sem=PE if s == 3 else None)
            P.wait("vector", PE, vm)
            for s in range(4):
                P.need("vector", ("idxi", par, s))
                P.need("vector", ("idxiY", par, s))
            P.need("vector", ("idxiG", par))
            dve("tensor_copy", idxf[:, par * 4:par * 4 + 4], ps_s[:, 448:452])
            vq = dve("tensor_copy", idxi[:, par * 4:par * 4 + 4],
                     idxf[:, par * 4:par * 4 + 4], sem=DVE)
            P.consumed("ps_s", DVE, vq)
            P.consumed(("idxrow", par), DVE, vq)
            P.produced(("idxiok", par), DVE, vq)

        def gather_chunk(j, par):
            ve = None
            for s in range(4):
                gb = s % 2
                P.need("gpsimd", ("xg_tm", gb))
                P.waitr("gpsimd", ("idxiok", par))
                vg = idma(xg_tm[gb][:], None, xn2_d,
                          IndirectOffsetOnAxis(idxi[:, par * 4 + s:par * 4 + s + 1], 0),
                          NTOK - 1, GX[gb])
                P.produced(("xg", gb), GX[gb], vg)
                P.waitr("tensor", ("xg", gb))
                P.need("tensor", "ps_tr")
                for c in range(8):
                    vt = tr(ps_tr[:, c * 128:(c + 1) * 128],
                            xg_tm[gb][:, c * 128:(c + 1) * 128],
                            sem=PE if c == 7 else None)
                P.consumed(("xg_tm", gb), PE, vt)
                P.wait("vector", PE, vt)
                P.need("vector", ("xgT", par))
                dst = xgT[par].ap().rearrange("p (c t) -> p c t", c=8)[:, :, s * 128:(s + 1) * 128]
                src = ps_tr.ap().rearrange("p (c t) -> p c t", c=8)
                ve = dve("tensor_copy", dst, src, sem=DVE)
                P.consumed("ps_tr", DVE, ve)
            P.produced(("xgTok", par), DVE, ve)

        def load_bias(j, par):
            P.need("gpsimd", ("bias", par))
            P.need("gpsimd", ("biasd", par))
            P.waitr("gpsimd", "bidxall")
            vb = idma(bias_sb[par][:], None, fbias_flat,
                      IndirectOffsetOnAxis(SB["bidxi"][:, j:j + 1], 0),
                      NE * 128 - 1, GB[par])
            P.produced(("biasok", par), GB[par], vb)

        def wload(k):
            slot = k % 4
            P.need("gpsimd", ("wbuf", slot))
            P.waitr("gpsimd", "widxall")
            vw = idma(wbuf[slot][:], None, fw_flat,
                      IndirectOffsetOnAxis(SB["widxi"][:, k:k + 1], 0),
                      NE * NFB * 128 - 1, LW[slot])
            P.produced(("w", slot), LW[slot], vw)

        def h_phase(j, fb, wb, zb, par):
            P.waitr("tensor", ("w", wb))
            P.waitr("tensor", ("xgTok", par))
            P.waitr("scalar", ("biasok", par))
            vh = None
            for fc in range(4):
                hp = fc % 2
                P.need("tensor", ("ps_h", hp))
                for c in range(8):
                    vh = mm(ps_h[hp][:],
                            wbuf[wb][:, c * 512 + fc * 128: c * 512 + (fc + 1) * 128],
                            xgT[par][:, c * 512:(c + 1) * 512],
                            start=(c == 0), stop=(c == 7),
                            sem=PE if c == 7 else None)
                P.wait("scalar", PE, vh)
                P.need("scalar", ("z", zb, fc))
                va = act(z_sb[zb][:, fc * 512:(fc + 1) * 512], ps_h[hp][:],
                         AF.Relu, bias=bias_sb[par][:, fb * 4 + fc: fb * 4 + fc + 1],
                         sem=ACT)
                P.consumed(("ps_h", hp), ACT, va)
                P.produced(("zr", zb, fc), ACT, va)
            return vh

        def y_phase(j, fb, wb, zb, par):
            for fc in range(4):
                P.waitr("tensor", ("zr", zb, fc))
            P.waitr("vector", ("biasok", par))
            vy = None
            vv = None
            for dc in range(8):
                pb = dc % 4
                P.need("tensor", ("ps_y", pb))
                for jj in range(4):
                    vy = mm(ybank[pb],
                            wbuf[wb][:, 4096 + jj * 1024 + dc * 128:
                                     4096 + jj * 1024 + (dc + 1) * 128],
                            z_sb[zb][:, jj * 512:(jj + 1) * 512],
                            start=(jj == 0), stop=(jj == 3),
                            sem=PE if jj == 3 else None)
                P.wait("vector", PE, vy)
                dstp = y_sb[par][:, dc * 512:(dc + 1) * 512]
                if fb == 0:
                    P.need("vector", ("y", par))
                    P.need("vector", ("yB", par))
                    vv = dve("tensor_scalar_add", dstp, ybank[pb],
                             bias_sb[par][:, 32 + dc:33 + dc], sem=DVE)
                else:
                    vv = dve("tensor_tensor", dstp, dstp, ybank[pb], ALU.add, sem=DVE)
                P.consumed(("ps_y", pb), DVE, vv)
            for fc in range(4):
                P.consumed(("z", zb, fc), PE, vy)
            P.consumed(("wbuf", wb), PE, vy)
            P.produced(("yok", par), DVE, vv)
            if fb == 0:
                P.consumed(("biasd", par), DVE, vv)
            return vy

        def scatter_back(j, par):
            P.waitr("tensor", ("yok", par))
            vt3 = None
            va_sq = None
            vgg = None
            for s in range(4):
                P.need("tensor", "ps_num")
                P.need("tensor", ("ps_y", 2))
                P.need("tensor", ("ps_y", 3))
                for dc in range(8):
                    vt = tr(ps_num[:, dc * 128:(dc + 1) * 128],
                            y_sb[par][:, dc * 512 + s * 128: dc * 512 + (s + 1) * 128],
                            fp32=True, sem=PE if dc == 7 else None)
                vt3 = vt
                P.wait("vector", PE, vt)
                ve = dve("tensor_copy", ysc[s][:], ps_num[:, 0:1024], sem=DVE)
                P.consumed("ps_num", DVE, ve)
                P.consumed(("ps_y", 2), DVE, ve)
                P.consumed(("ps_y", 3), DVE, ve)
                dve("tensor_reduce", sums[:, s:s + 1], ysc[s][:], axis=AX.X, op=ALU.add)
                P.wait("scalar", DVE, ve)
                va_sq = act(sqscr[:], ysc[s][:], AF.Square, accum=ssq[:, s:s + 1],
                            sem=ACT)
                P.waitr("gpsimd", ("idxiok", par))
                if s == 0:
                    P.need("gpsimd", ("gbuf", par))
                vgg = idma(gbuf[:, par * 4 + s:par * 4 + s + 1], None, g_d,
                           IndirectOffsetOnAxis(idxi[:, par * 4 + s:par * 4 + s + 1], 0),
                           NTOK - 1, GG[par])
                P.need("gpsimd", ("y1t", s))
                vy1 = idma(y1t[s][:], None, y1_d,
                           IndirectOffsetOnAxis(idxi[:, par * 4 + s:par * 4 + s + 1], 0),
                           NTOK - 1, GY[s])
                P.produced(("y1g", s), GY[s], vy1)
                P.consumed(("idxiY", par, s), GY[s], vy1)
            P.consumed(("y", par), PE, vt3)
            P.consumed(("yB", par), PE, vt3)
            P.consumed(("idxiG", par), GG[par], vgg)
            P.wait("vector", ACT, va_sq)
            P.wait("vector", GG[par], vgg)
            dve("tensor_scalar_mul", mcol[:, 0:4], sums[:, 0:4], 1.0 / D)
            dve("tensor_scalar_mul", varc[:, 0:4], ssq[:, 0:4], 1.0 / D)
            dve("tensor_tensor", lgscr[:, 40:44], mcol[:, 0:4], mcol[:, 0:4], ALU.mult)
            dve("tensor_tensor", varc[:, 0:4], varc[:, 0:4], lgscr[:, 40:44],
                ALU.subtract)
            gq = gbuf[:, par * 4:par * 4 + 4]
            dve("tensor_tensor", SB["g2"][:], gq, gq, ALU.mult)
            vg2 = dve("tensor_tensor", varc[:, 0:4], varc[:, 0:4], SB["g2"][:],
                      ALU.mult, sem=DVE)
            P.consumed(("gbuf", par), DVE, vg2)
            v = dve("tensor_scalar_add", varc[:, 0:4], varc[:, 0:4], EPS, sem=DVE)
            P.wait("scalar", DVE, v)
            va = act(lgscr[:, 48:52], varc[:, 0:4], AF.Sqrt, sem=ACT)
            P.wait("vector", ACT, va)
            dve("reciprocal", rstd[:, 0:4], lgscr[:, 48:52])
            dve("tensor_tensor", SB["grstd"][:], gq, rstd[:, 0:4], ALU.mult)
            dve("tensor_tensor", nbias[:, 0:4], mcol[:, 0:4], SB["grstd"][:], ALU.mult)
            dve("tensor_scalar_mul", nbias[:, 0:4], nbias[:, 0:4], -1.0)
            for s in range(4):
                dve("tensor_scalar", ysc[s][:], ysc[s][:], SB["grstd"][:, s:s + 1],
                    nbias[:, s:s + 1], ALU.mult, ALU.add)
                P.waitr("vector", ("y1g", s))
                vadd = dve("tensor_tensor", y1t[s][:], y1t[s][:], ysc[s][:], ALU.add,
                           sem=DVE)
                P.wait("gpsimd", DVE, vadd)
                vout = idma(out_flat,
                            IndirectOffsetOnAxis(idxi[:, par * 4 + s:par * 4 + s + 1], 0),
                            y1t[s][:], None, NTOK - 1, OS[par][s])
                P.consumed(("y1t", s), OS[par][s], vout)
                P.consumed(("idxi", par, s), OS[par][s], vout)

        ybank = [ps_wc[0].ap()[:], ps_wc[1].ap()[:],
                 ps_num.ap()[:, 0:512], ps_num.ap()[:, 512:1024]]

        NPAIR = NCHUNK * NFB
        load_chunk_indices(0, 0)
        gather_chunk(0, 0)
        load_bias(0, 0)
        for kk in range(min(3, NPAIR)):
            wload(kk)
        prev = None
        for k in range(NPAIR):
            j, fb = divmod(k, NFB)
            par = j % 2
            if fb == 4 and j + 1 < NCHUNK:
                load_chunk_indices(j + 1, (j + 1) % 2)
                gather_chunk(j + 1, (j + 1) % 2)
                load_bias(j + 1, (j + 1) % 2)
            vh_last = h_phase(j, fb, k % 4, k % 2, par)
            if fb == NFB - 1:
                P.consumed(("xgT", par), PE, vh_last)
                P.consumed(("bias", par), ACT, ACT.n)
            if prev is not None:
                jp, fbp = divmod(prev, NFB)
                y_phase(jp, fbp, prev % 4, prev % 2, jp % 2)
                if fbp == NFB - 1:
                    scatter_back(jp, jp % 2)
            if k + 3 < NPAIR:
                wload(k + 3)
            prev = k
        jp, fbp = divmod(prev, NFB)
        y_phase(jp, fbp, prev % 4, prev % 2, jp % 2)
        scatter_back(jp, jp % 2)

        stagecut(6, [("dbg_moe", moe_d), ("dbg_inv", inv_d), ("dbg_xn2", xn2_d),
                     ("dbg_g", gcol[:]), ("dbg_t1", t1cols[:])])

        for par in range(2):
            for s in range(4):
                P.wait("sync", OS[par][s], OS[par][s].n)

    def fence():
        for eng in ENGS:
            P.wait(eng, PE, PE.n)
            P.wait(eng, ACT, ACT.n)
            P.wait(eng, DVE, DVE.n)

    open_stacks = []
    try:
        esa = ExitStack()
        open_stacks.append(esa)
        sba = lambda n, s, dt: esa.enter_context(nc.sbuf_tensor(n, s, dt))
        SA = {
            "stream": sba("stream", [128, NT * 1024], F32),
            "xnT": sba("xnT", [128, 8 * T], BF16),
            "w1t": sba("w1t", [128, 8192], BF16),
            "wct": sba("wct", [128, 16384], BF16),
        }
        P.wait("gpsimd", LWA, LWA.n)
        vw1 = dma("gpsimd", SA["w1t"][:], h_in["w1t_p"], LWA)
        vwc = dma("gpsimd", SA["wct"][:], h_in["wct_p"], LWA)
        P.produced("w1t", LWA, vwc)
        P.produced("wct", LWA, vwc)
        for b in range(BPC):
            esb = ExitStack()
            open_stacks.append(esb)
            sbb = lambda n, s, dt: esb.enter_context(nc.sbuf_tensor(f"{n}_{b}", s, dt))
            AB = {
                "xn_tmp": sbb("xn_tmp", [128, 2048], BF16),
                "exnfm": sbb("exnfm", [128, 1024], BF16),
                "exntm": sbb("exntm", [128, 1024], BF16),
                "hsb": sbb("hsb", [128, 1024], BF16),
                "afm": sbb("afm", [128, 1024], BF16),
                "atm": sbb("atm", [128, 4096], BF16),
                "ebc": sbb("ebc", [128, 512], BF16),
                "ctxT": [sbb("ctxT0", [128, 4096], BF16),
                         sbb("ctxT1", [128, 4096], BF16)],
            }
            fence()
            phase_a(b, AB, SA)
            esb.close()
            open_stacks.remove(esb)
        esa.close()
        open_stacks.remove(esa)

        esf = ExitStack()
        open_stacks.append(esf)
        sbf = lambda n, s, dt: esf.enter_context(nc.sbuf_tensor(n, s, dt))
        SB = {
            "wbuf": [sbf(f"wbuf{i}", [128, 8192], BF16) for i in range(4)],
            "y_sb": [sbf("ysb0", [128, 4096], F32), sbf("ysb1", [128, 4096], F32)],
            "xg_tm": [sbf("xgtm0", [128, 1024], BF16), sbf("xgtm1", [128, 1024], BF16)],
            "xgT": [sbf("xgT0", [128, 4096], BF16), sbf("xgT1", [128, 4096], BF16)],
            "ysc": [sbf("ysc0", [128, 1024], F32), sbf("ysc1", [128, 1024], F32)],
            "z": [sbf("z0", [128, 2048], BF16), sbf("z1", [128, 2048], BF16)],
            "bias": [sbf("bias0", [128, 40], F32), sbf("bias1", [128, 40], F32)],
            "idxf": sbf("idxf", [128, 8], F32),
            "idxi": sbf("idxi", [128, 8], I32),
            "idxrow": [sbf("idxrow0", [1, 512], F32), sbf("idxrow1", [1, 512], F32)],
            "moet": [sbf(f"moet{i}", [128, 1024], F32) for i in range(4)],
            "y1t": [sbf(f"y1t{i}", [128, 1024], F32) for i in range(4)],
            "destf": sbf("destf", [128, 1], F32),
            "desti": sbf("desti", [128, 1], I32),
            "tokf": sbf("tokf", [128, 1], F32),
            "capi": sbf("capi", [128, 1], I32),
            "cmp16": sbf("cmp16", [8, 16], F32),
            "widxf": sbf("widxf", [128, 128], F32),
            "widxi": sbf("widxi", [128, 128], I32),
            "bidxf": sbf("bidxf", [128, 16], F32),
            "bidxi": sbf("bidxi", [128, 16], I32),
            "g2": sbf("g2", [128, 4], F32),
            "grstd": sbf("grstd", [128, 4], F32),
        }
        fence()
        phase_ffn(SB)
        esf.close()
        open_stacks.remove(esf)
    except StageStop:
        for st_ in reversed(open_stacks):
            st_.close()

    P.wait("sync", ST, ST.n)
    for _so in SO:
        P.wait("sync", _so, _so.n)

    with nc.Block() as block:
        def run(eng_items):
            def f(E):
                for fn in eng_items:
                    fn(E)
            return f
        block.sync(run(P.items["sync"]))
        block.tensor(run(P.items["tensor"]))
        block.vector(run(P.items["vector"]))
        block.scalar(run(P.items["scalar"]))
        block.gpsimd(run(P.items["gpsimd"]))

    es.close()
    return nc


_NC_CACHE = {}


def kernel(**inputs):
    if "nc" not in _NC_CACHE:
        _NC_CACHE["nc"] = build()
    nc = _NC_CACHE["nc"]
    packed = pack_host(inputs)
    x = np.asarray(inputs["L0_attn_x"], np.float32)
    in_maps = []
    for c in range(NCORES):
        m = dict(packed)
        m["x"] = np.ascontiguousarray(x[2 * c:2 * c + 2])
        in_maps.append(m)
    res = run_bass_kernel_spmd(nc, in_maps, core_ids=list(range(NCORES)))
    return np.concatenate([r["out"] for r in res.results], axis=0)


# version 17
# speedup vs baseline: 1.0650x; 1.0650x over previous
import numpy as np
import ml_dtypes
from contextlib import ExitStack

import concourse.bass as bass
import concourse.mybir as mybir
from concourse.bass import IndirectOffsetOnAxis
from concourse.bass_utils import run_bass_kernel_spmd

F32 = mybir.dt.float32
BF16 = mybir.dt.bfloat16
I32 = mybir.dt.int32
AF = mybir.ActivationFunctionType
ALU = mybir.AluOpType
AX = mybir.AxisListType
NPBF = ml_dtypes.bfloat16

B_TOT, T, D = 16, 2048, 1024
NCORES = 8
BPC = 2
NT = 16
NG = 4
DFF = 4096
NE = 8
FB = 512
NFB = DFF // FB
NTOK = BPC * T
CH = 512
NCHUNK = NTOK // CH + NE
NSLOT = NCHUNK * CH
EPS = 1e-5

ENGS = ("sync", "tensor", "vector", "scalar", "gpsimd")


class Sem:
    def __init__(self, handle, name):
        self.h = handle
        self.n = 0
        self.name = name


class Plan:
    def __init__(self):
        self.items = {e: [] for e in ENGS}
        self.ready = {}
        self.guard = {}

    def op(self, eng, fn, sem=None, inc=1):
        if sem is not None:
            sem.n += inc
            v = sem.n
            self.items[eng].append(lambda E, fn=fn, s=sem, k=inc: fn(E).then_inc(s.h, k))
            return v
        self.items[eng].append(fn)
        return None

    def wait(self, eng, sem, val):
        if val is None or val <= 0:
            return
        self.items[eng].append(lambda E, s=sem, v=val: E.wait_ge(s.h, v))

    def produced(self, key, sem, val):
        self.ready[key] = (sem, val)

    def waitr(self, eng, key):
        if key in self.ready:
            self.wait(eng, *self.ready[key])

    def consumed(self, key, sem, val):
        self.guard[key] = (sem, val)

    def need(self, eng, key):
        if key in self.guard:
            self.wait(eng, *self.guard[key])


def pack_host(inputs):
    f32 = np.float32
    W1 = np.asarray(inputs["L0_attn_W1"], f32)
    w2 = np.asarray(inputs["L0_attn_w2"], f32)
    Wc = np.asarray(inputs["L0_attn_Wc"], f32)
    b1 = np.asarray(inputs["L0_attn_b1"], f32)
    bc = np.asarray(inputs["L0_attn_bc"], f32)
    b2 = np.asarray(inputs["L0_attn_b2"], f32)
    rW = np.asarray(inputs["L0_ffn_router_W"], f32)
    rb = np.asarray(inputs["L0_ffn_router_b"], f32)
    fW1 = np.asarray(inputs["L0_ffn_W1"], f32)
    fb1 = np.asarray(inputs["L0_ffn_b1"], f32)
    fW2 = np.asarray(inputs["L0_ffn_W2"], f32)
    fb2 = np.asarray(inputs["L0_ffn_b2"], f32)

    d = {}
    d["w1t_p"] = np.ascontiguousarray(
        W1.T.reshape(8, 128, 1024).transpose(1, 0, 2).reshape(128, 8192))
    d["wct_p"] = np.ascontiguousarray(
        Wc.T.reshape(16, 128, 1024).transpose(1, 0, 2).reshape(128, 16384))
    d["w2c_p"] = np.ascontiguousarray(w2[0].reshape(8, 128).T)
    d["b1a_p"] = np.ascontiguousarray(b1.reshape(8, 128).T)
    d["bca_p"] = np.ascontiguousarray(bc.reshape(8, 128).T)
    d["scs_p"] = np.zeros((1, 8), f32)
    d["scs_p"][0, 0] = b2[0]
    d["rwcf_p"] = np.ascontiguousarray(
        rW.reshape(8, 128, 8).transpose(1, 0, 2).reshape(128, 64))
    d["rbc_p"] = np.ascontiguousarray(np.broadcast_to(rb[None, :], (128, 8)))
    w1p = fW1.reshape(8, 8, 128, 8, 512).transpose(0, 3, 2, 1, 4).reshape(8, 8, 128, 4096)
    w2p = fW2.reshape(8, 8, 4, 128, 1024).transpose(0, 1, 3, 2, 4).reshape(8, 8, 128, 4096)
    d["fw_p"] = np.ascontiguousarray(np.concatenate([w1p, w2p], axis=3))
    fbias = np.concatenate([
        fb1.reshape(8, 32, 128).transpose(0, 2, 1),
        fb2.reshape(8, 8, 128).transpose(0, 2, 1),
    ], axis=2)
    d["fbias_p"] = np.ascontiguousarray(fbias)
    idn = np.eye(128)
    d["idb_c"] = idn.astype(NPBF)
    d["idf_c"] = idn.astype(f32)
    U = np.triu(np.ones((128, 128)))
    d["Ub_c"] = U.astype(NPBF)
    d["Uf_c"] = U.astype(f32)
    d["onesb_c"] = np.ones((128, 128), NPBF)
    d["onesf_c"] = np.ones((128, 128), f32)
    d["iota8_c"] = np.broadcast_to(np.arange(8, dtype=f32)[None, :], (128, 8)).copy()
    d["iotac_c"] = (np.arange(128, dtype=f32) % 8)[:, None].copy()
    d["iotap_c"] = np.arange(128, dtype=f32)[:, None].copy()
    d["iota16_c"] = np.broadcast_to(np.arange(16, dtype=f32)[None, :], (8, 16)).copy()
    d["oob_c"] = np.full((NSLOT, 1), 1.0e9, f32)
    fbpat = np.zeros((128, 128), f32)
    for jj in range(16):
        for fb in range(8):
            fbpat[:, jj * 8 + fb] = fb * 128 + np.arange(128)
    d["fbpat_c"] = fbpat
    d["u32s_c"] = np.triu(np.ones((32, 32)), 1).astype(f32)
    d["tcol_c"] = np.broadcast_to(np.arange(32, dtype=f32)[None, :], (128, 32)).copy()
    tok = np.zeros((128, 32), f32)
    for t in range(32):
        tok[:, t] = (t // 16) * T + (t % 16) * 128 + np.arange(128)
    d["tokall_c"] = tok
    return d


def build(stage=0):
    nc = bass.Bass("TRN2", target_bir_lowering=False)
    P = Plan()

    x_h = nc.dram_tensor("x", [BPC, T, D], F32, kind="ExternalInput").ap()
    out_h = nc.dram_tensor("out", [BPC, T, D], F32, kind="ExternalOutput").ap()
    h_in = {}
    for name, shape in [
        ("w1t_p", [128, 8192]), ("wct_p", [128, 16384]), ("w2c_p", [128, 8]),
        ("b1a_p", [128, 8]), ("bca_p", [128, 8]), ("scs_p", [1, 8]),
        ("rwcf_p", [128, 64]), ("rbc_p", [128, 8]),
        ("fw_p", [NE, NFB, 128, 8192]), ("fbias_p", [NE, 128, 40]),
        ("idf_c", [128, 128]), ("Uf_c", [128, 128]),
        ("onesf_c", [128, 128]), ("iota8_c", [128, 8]), ("iotac_c", [128, 1]),
        ("iotap_c", [128, 1]), ("iota16_c", [8, 16]), ("oob_c", [NSLOT, 1]),
        ("fbpat_c", [128, 128]), ("u32s_c", [32, 32]),
        ("tcol_c", [128, 32]), ("tokall_c", [128, 32]),
    ]:
        h_in[name] = nc.dram_tensor(name, shape, F32, kind="ExternalInput").ap()
    for name, shape in [
        ("idb_c", [128, 128]), ("Ub_c", [128, 128]),
        ("onesb_c", [128, 128]),
    ]:
        h_in[name] = nc.dram_tensor(name, shape, BF16, kind="ExternalInput").ap()

    xn2_d = nc.dram_tensor("xn2_d", [NTOK, D], BF16, kind="Internal").ap()
    y1_d = nc.dram_tensor("y1_d", [NTOK, D], F32, kind="Internal").ap()
    inv_d = nc.dram_tensor("inv_d", [NSLOT, 1], F32, kind="Internal").ap()
    moe_d = nc.dram_tensor("moe_d", [NTOK, D], F32, kind="Internal").ap()
    baseflat = nc.dram_tensor("baseflat_d", [256, 1], F32, kind="Internal").ap()
    g_d = nc.dram_tensor("g_d", [NTOK, 1], F32, kind="Internal").ap()
    fw_flat = h_in["fw_p"].rearrange("e f p c -> (e f p) c")
    fbias_flat = h_in["fbias_p"].rearrange("e p c -> (e p) c")

    dbg = {}
    if stage:
        for nm, shp, dt in [("dbg_stream", [128, 16384], F32), ("dbg_xnT", [128, 16384], BF16),
                            ("dbg_rows", [128, 2048], F32), ("dbg_etm", [128, 16], F32),
                            ("dbg_g", [128, 32], F32), ("dbg_t1", [128, 32], BF16),
                            ("dbg_m8", [128, 8], F32), ("dbg_inv", [NSLOT, 1], F32),
                            ("dbg_xn2", [NTOK, D], BF16), ("dbg_moe", [NTOK, D], F32)]:
            dbg[nm] = nc.dram_tensor(nm, shp, dt, kind="ExternalOutput").ap()

    es = ExitStack()

    LC = Sem(es.enter_context(nc.semaphore("LC")), "LC")
    LX = Sem(es.enter_context(nc.semaphore("LX")), "LX")
    LWA = Sem(es.enter_context(nc.semaphore("LWA")), "LWA")
    LW = [Sem(es.enter_context(nc.semaphore(f"LW{i}")), f"LW{i}") for i in range(4)]
    GX = [Sem(es.enter_context(nc.semaphore("GX0")), "GX0"),
          Sem(es.enter_context(nc.semaphore("GX1")), "GX1")]
    GI = [Sem(es.enter_context(nc.semaphore("GI0")), "GI0"),
          Sem(es.enter_context(nc.semaphore("GI1")), "GI1")]
    GB = [Sem(es.enter_context(nc.semaphore("GB0")), "GB0"),
          Sem(es.enter_context(nc.semaphore("GB1")), "GB1")]
    GS = Sem(es.enter_context(nc.semaphore("GS")), "GS")
    GS2 = [Sem(es.enter_context(nc.semaphore(f"GS2{i}")), f"GS2{i}") for i in range(2)]
    C8S = Sem(es.enter_context(nc.semaphore("C8S")), "C8S")
    GD = Sem(es.enter_context(nc.semaphore("GD")), "GD")
    BFD = Sem(es.enter_context(nc.semaphore("BFD")), "BFD")
    BSG = Sem(es.enter_context(nc.semaphore("BSG")), "BSG")
    MT = [Sem(es.enter_context(nc.semaphore(f"MTm{i}")), f"MTm{i}") for i in range(4)]
    MT2 = [Sem(es.enter_context(nc.semaphore(f"MTy{i}")), f"MTy{i}") for i in range(4)]
    XT = [Sem(es.enter_context(nc.semaphore("XT0")), "XT0"),
          Sem(es.enter_context(nc.semaphore("XT1")), "XT1")]
    Y1S = Sem(es.enter_context(nc.semaphore("Y1S")), "Y1S")
    SO = [Sem(es.enter_context(nc.semaphore(f"SO{i}")), f"SO{i}") for i in range(4)]
    ST = Sem(es.enter_context(nc.semaphore("ST")), "ST")
    PE = Sem(es.enter_context(nc.semaphore("PE")), "PE")
    ACT = Sem(es.enter_context(nc.semaphore("ACT")), "ACT")
    DVE = Sem(es.enter_context(nc.semaphore("DVE")), "DVE")

    sb = lambda n, s, dt: es.enter_context(nc.sbuf_tensor(n, s, dt))
    rows_a = sb("rows_a", [128, T], F32)
    tmp32 = sb("tmp32", [128, 1024], F32)
    sqscr = sb("sqscr", [128, 1024], BF16)
    recbc = sb("recbc", [128, 128], F32)
    e_tm = sb("e_tm", [128, 16], F32)
    sums = sb("sums", [128, 16], F32)
    ssq = sb("ssq", [128, 16], F32)
    mcol = sb("mcol", [128, 16], F32)
    varc = sb("varc", [128, 16], F32)
    rstd = sb("rstd", [128, 16], F32)
    nbias = sb("nbias", [128, 16], F32)
    rstd3 = sb("rstd3", [128, 16], F32)
    nbias3 = sb("nbias3", [128, 16], F32)
    gcol = sb("gcol", [128, 32], F32)
    t1cols = sb("t1cols", [128, 32], BF16)
    t1f = sb("t1f", [128, 32], F32)
    lgscr = sb("lgscr", [128, 64], F32)
    ncarry = sb("ncarry", [128, 8], F32)
    scs = sb("scs", [1, 8], F32)
    xn2fT = sb("xn2fT", [128, 1024], F32)
    idb = sb("idb", [128, 128], BF16)
    idf = sb("idf", [128, 128], F32)
    Ub = sb("Ub", [128, 128], BF16)
    Uf = sb("Uf", [128, 128], F32)
    onesb = sb("onesb", [128, 128], BF16)
    onesf = sb("onesf", [128, 128], F32)
    iota8 = sb("iota8", [128, 8], F32)
    iotac = sb("iotac", [128, 1], F32)
    iotap = sb("iotap", [128, 1], F32)
    iota16 = sb("iota16", [8, 16], F32)
    rbc = sb("rbc", [128, 8], F32)
    w2c = sb("w2c", [128, 8], BF16)
    b1a = sb("b1a", [128, 8], F32)
    bca = sb("bca", [128, 8], F32)
    rwcf = sb("rwcf", [128, 64], F32)
    fbpat = sb("fbpat", [128, 128], F32)
    m8 = sb("m8", [128, 8], F32)
    echunk_bc = sb("echunk_bc", [128, 16], F32)
    oh_all = sb("oh_all", [128, 256], F32)
    c8T = sb("c8T", [32, 8], F32)
    cntsb = sb("cntsb", [128, 32], F32)
    rkm = sb("rkm", [128, 32], F32)
    rank_all = sb("rank_all", [128, 32], F32)
    bsel_all = sb("bsel_all", [128, 32], F32)
    base32 = sb("base32", [8, 32], F32)
    idxgf = sb("idxgf", [128, 32], F32)
    idxgi = sb("idxgi", [128, 32], I32)
    dest32f = sb("dest32f", [128, 32], F32)
    dest32i = sb("dest32i", [128, 32], I32)
    tcol = sb("tcol", [128, 32], F32)
    tokall = sb("tokall", [128, 32], F32)
    u32s = sb("u32s", [32, 32], F32)
    gbuf = sb("gbuf", [128, 8], F32)

    pst = lambda n, s, dt: es.enter_context(nc.psum_tensor(n, s, dt))
    ps_tr = pst("ps_tr", [128, 1024], BF16)
    ps_h = [pst("ps_h0", [128, 512], F32), pst("ps_h1", [128, 512], F32)]
    ps_s = pst("ps_s", [128, 512], F32)
    ps_num = pst("ps_num", [128, 1024], F32)
    ps_wc = [pst("ps_wc0", [128, 512], F32), pst("ps_wc1", [128, 512], F32)]

    def dma(queue_eng, out_ap, in_ap, sem):
        return P.op(queue_eng, lambda E, o=out_ap, i=in_ap: E.dma_start(out=o, in_=i), sem, 16)

    _bc_regs = {}

    def idma(out_ap, out_off, in_ap, in_off, bc, sem):
        def fn(E, o=out_ap, oo=out_off, i=in_ap, io=in_off, b=bc):
            if b not in _bc_regs:
                _bc_regs[b] = E.to_reg(b)
            return E.indirect_dma_start(o, oo, i, io, bounds_check=_bc_regs[b],
                                        oob_is_err=False)
        return P.op("gpsimd", fn, sem, 16)

    def mm(out_ap, lh, rh, start, stop, sem=None):
        return P.op("tensor",
                    lambda t, o=out_ap, a=lh, b=rh, s1=start, s2=stop:
                    t.matmul(o, a, b, start=s1, stop=s2), sem, 1)

    def tr(out_ap, in_ap, fp32=False, sem=None):
        ident = (idf if fp32 else idb)[:]
        return P.op("tensor",
                    lambda t, o=out_ap, i=in_ap, d=ident: t.transpose(o, i, d), sem, 1)

    def act(out_ap, in_ap, f, bias=0.0, scale=1.0, accum=None, sem=None):
        P.wait("scalar", ACT, ACT.n)
        return P.op("scalar",
                    lambda s, o=out_ap, i=in_ap, ff=f, bb=bias, sc=scale, au=accum:
                    s.activation(o, i, ff, bias=bb, scale=sc, accum_out=au), ACT, 1)

    def dve(method, *aps, sem=None, **kw):
        def fn(v, m=method, a=aps, kw2=kw):
            return getattr(v, m)(*a, **kw2)
        P.wait("vector", DVE, DVE.n)
        return P.op("vector", fn, DVE, 1)

    for name, dst in [("idf_c", idf), ("Uf_c", Uf), ("onesf_c", onesf),
                      ("iota8_c", iota8), ("iotac_c", iotac), ("iotap_c", iotap),
                      ("iota16_c", iota16), ("rwcf_p", rwcf), ("fbpat_c", fbpat),
                      ("rbc_p", rbc), ("b1a_p", b1a),
                      ("bca_p", bca), ("scs_p", scs),
                      ("idb_c", idb), ("Ub_c", Ub), ("onesb_c", onesb),
                      ("u32s_c", u32s), ("tcol_c", tcol), ("tokall_c", tokall)]:
        dma("sync", dst[:], h_in[name], LC)
    dma("sync", inv_d, h_in["oob_c"], LC)
    v_consts = LC.n
    dma("gpsimd", w2c[:], h_in["w2c_p"], LWA)
    v_smallw = LWA.n
    for eng in ENGS:
        P.wait(eng, LC, v_consts)
        P.wait(eng, LWA, v_smallw)

    class StageStop(Exception):
        pass

    def stagecut(k, tap_pairs):
        if stage == k:
            for s_ in (PE, ACT, DVE, LX, LWA, LW[0], LW[1], GS, LC,
                       XT[0], XT[1], Y1S, C8S, BFD, BSG):
                P.wait("sync", s_, s_.n)
            for nm, ap in tap_pairs:
                dma("sync", dbg[nm], ap, ST)
            raise StageStop()

    def ln_stats_tile(src_ap, col, ready_key):
        P.waitr("vector", ready_key)
        dve("tensor_reduce", sums[:, col:col + 1], src_ap, sem=None,
            axis=AX.X, op=ALU.add)
        P.waitr("scalar", ready_key)
        act(sqscr[:, 0:src_ap.shape[1]], src_ap, AF.Square,
            accum=ssq[:, col:col + 1], sem=ACT)

    def ln_rstd(cols, n_elem, tag, out_r=None, out_b=None, ocols=None):
        a, b = cols.start, cols.stop
        oa, ob = (a, b) if ocols is None else (ocols.start, ocols.stop)
        out_r = rstd if out_r is None else out_r
        out_b = nbias if out_b is None else out_b
        w = b - a
        P.wait("vector", ACT, ACT.n)
        dve("tensor_scalar_mul", mcol[:, a:b], sums[:, a:b], 1.0 / n_elem)
        dve("tensor_scalar_mul", varc[:, a:b], ssq[:, a:b], 1.0 / n_elem)
        dve("tensor_tensor", lgscr[:, 40:40 + w], mcol[:, a:b], mcol[:, a:b], ALU.mult)
        dve("tensor_tensor", varc[:, a:b], varc[:, a:b], lgscr[:, 40:40 + w],
            ALU.subtract)
        v = dve("tensor_scalar_add", varc[:, a:b], varc[:, a:b], EPS, sem=DVE)
        P.wait("scalar", DVE, v)
        va = act(lgscr[:, 48:48 + w], varc[:, a:b], AF.Sqrt, sem=ACT)
        P.wait("vector", ACT, va)
        dve("reciprocal", out_r[:, oa:ob], lgscr[:, 48:48 + w])
        dve("tensor_tensor", out_b[:, oa:ob], mcol[:, a:b], out_r[:, oa:ob], ALU.mult)
        v = dve("tensor_scalar_mul", out_b[:, oa:ob], out_b[:, oa:ob], -1.0, sem=DVE)
        P.produced(("rstd", tag), DVE, v)

    def phase_a(b, AB, SA):
        xn_tmp, exnfm, exntm, hsb, afm, atm, ebc, ctxT = (
            AB["xn_tmp"], AB["exnfm"], AB["exntm"], AB["hsb"], AB["afm"],
            AB["atm"], AB["ebc"], AB["ctxT"])
        stream, xnT, w1t, wct = SA["stream"], SA["xnT"], SA["w1t"], SA["wct"]

        if b > 0:
            P.wait("sync", Y1S, P.ready[("y1stored", b - 1)][1])
        for i in range(NT):
            v = dma("sync", stream[:, i * 1024:(i + 1) * 1024],
                    x_h[b, i * 128:(i + 1) * 128, :], LX)
        for i in range(NT):
            P.produced(("x", b, i), LX, v)

        for i in range(NT):
            ln_stats_tile(stream[:, i * 1024:(i + 1) * 1024], i, ("x", b, i))
        ln_rstd(slice(0, NT), D, "ln1")

        for i in range(NT):
            buf = i % 2
            xt = xn_tmp[:, buf * 1024:(buf + 1) * 1024]
            P.waitr("scalar", ("rstd", "ln1"))
            P.need("scalar", ("xn_tmp", buf))
            va = act(xt, stream[:, i * 1024:(i + 1) * 1024], AF.Identity,
                     bias=nbias[:, i:i + 1], scale=rstd[:, i:i + 1], sem=ACT)
            P.wait("tensor", ACT, va)
            P.need("tensor", "ps_tr")
            for c in range(8):
                vt = tr(ps_tr[:, c * 128:(c + 1) * 128], xt[:, c * 128:(c + 1) * 128],
                        sem=PE if c == 7 else None)
            P.consumed(("xn_tmp", buf), PE, vt)
            P.wait("vector", PE, vt)
            dst = xnT.ap().rearrange("p (c t) -> p c t", c=8)[:, :, i * 128:(i + 1) * 128]
            src = ps_tr.ap().rearrange("p (c t) -> p c t", c=8)
            ve = dve("tensor_copy", dst, src, sem=DVE)
            P.consumed("ps_tr", DVE, ve)
            P.produced(("xnT", b, i), DVE, ve)

        stagecut(1, [("dbg_stream", stream[:]), ("dbg_xnT", xnT[:])])

        for g in range(NG):
            for i in range(4 * g, 4 * g + 4):
                P.waitr("tensor", ("xnT", b, i))
            P.waitr("tensor", "w1t")
            va_tanh = [None] * 8
            vs = None
            for m in range(9):
                if m < 8:
                    pb = m % 2
                    P.need("tensor", ("ps_h", pb))
                    for c in range(8):
                        vh = mm(ps_h[pb][:],
                                w1t[:, c * 1024 + m * 128: c * 1024 + (m + 1) * 128],
                                xnT[:, c * T + g * 512: c * T + (g + 1) * 512],
                                start=(c == 0), stop=(c == 7),
                                sem=PE if c == 7 else None)
                    P.wait("scalar", PE, vh)
                    va_tanh[m] = act(hsb[:, pb * 512:(pb + 1) * 512], ps_h[pb][:],
                                     AF.Tanh, bias=b1a[:, m:m + 1], sem=ACT)
                    P.consumed(("ps_h", pb), ACT, va_tanh[m])
                if m >= 1:
                    mp = m - 1
                    P.wait("tensor", ACT, va_tanh[mp])
                    if mp == 0:
                        P.need("tensor", "ps_s")
                    vs = mm(ps_s[0:1, :], w2c[:, mp:mp + 1],
                            hsb[:, (mp % 2) * 512:((mp % 2) + 1) * 512],
                            start=(mp == 0), stop=(mp == 7), sem=PE)
                    P.consumed(("hsb", mp % 2), PE, vs)
            P.wait("vector", PE, vs)
            ve = dve("tensor_copy", rows_a[0:1, g * 512:(g + 1) * 512], ps_s[0:1, :], sem=DVE)
            P.consumed("ps_s", DVE, ve)

        dve("tensor_reduce", scs[0:1, 1:2], rows_a[0:1, :], axis=AX.X, op=ALU.max)
        v = dve("tensor_tensor", scs[0:1, 2:3], scs[0:1, 0:1], scs[0:1, 1:2],
                ALU.subtract, sem=DVE)
        P.wait("scalar", DVE, v)
        vex = act(rows_a[32:33, :], rows_a[0:1, :], AF.Exp, bias=scs[0:1, 2:3], sem=ACT)
        P.produced("e_row", ACT, vex)

        P.waitr("tensor", "e_row")
        P.need("tensor", "ps_s")
        for i in range(NT):
            vt = mm(ps_s[:, i:i + 1], rows_a[32:33, i * 128:(i + 1) * 128],
                    onesf[32:33, 0:1], True, True, sem=PE if i == NT - 1 else None)
        P.wait("vector", PE, vt)
        ve = dve("tensor_copy", e_tm[:], ps_s[:, 0:16], sem=DVE)
        P.consumed("ps_s", DVE, ve)
        P.produced("e_tm", DVE, ve)
        dve("memset", ncarry[:], 0.0)
        dve("memset", scs[0:1, 3:4], 0.0)

        stagecut(2, [("dbg_rows", rows_a[:]), ("dbg_etm", e_tm[:]),
                     ("dbg_xnT", xnT[:])])

        def cumsum_group(g):
            pb = g % 2
            ctxg = ctxT[pb]
            P.waitr("tensor", "e_row")
            P.need("tensor", "ps_num")
            vbq = mm(ps_num[:, 0:512], onesf[32:33, :],
                     rows_a[32:33, g * 512:(g + 1) * 512], True, True, sem=PE)
            P.wait("vector", PE, vbq)
            veb = dve("tensor_copy", ebc[:], ps_num[:, 0:512], sem=DVE)
            P.consumed("ps_num", DVE, veb)
            vc = None
            for k in range(4):
                i = 4 * g + k
                P.need("vector", "exnfm")
                for c in range(8):
                    vx = dve("tensor_tensor", exnfm[:, c * 128:(c + 1) * 128],
                             xnT[:, c * T + i * 128: c * T + (i + 1) * 128],
                             ebc[:, k * 128:(k + 1) * 128], ALU.mult,
                             sem=DVE if c == 7 else None)
                P.produced("exnfm", DVE, vx)
                P.waitr("tensor", "exnfm")
                P.need("tensor", "ps_tr")
                for c in range(8):
                    vt = tr(ps_tr[:, c * 128:(c + 1) * 128], exnfm[:, c * 128:(c + 1) * 128],
                            sem=PE if c == 7 else None)
                P.consumed("exnfm", PE, vt)
                P.wait("vector", PE, vt)
                P.need("vector", "exntm")
                ve = dve("tensor_copy", exntm[:], ps_tr[:, 0:1024], sem=DVE)
                P.consumed("ps_tr", DVE, ve)
                P.produced("exntm", DVE, ve)
                P.waitr("tensor", "e_tm")
                P.need("tensor", "ps_s")
                vd = mm(ps_s[0:1, 0:128], e_tm[:, i:i + 1], Uf[:], True, True, sem=PE)
                P.wait("vector", PE, vd)
                dve("tensor_scalar_add", rows_a[0:1, i * 128:(i + 1) * 128],
                    ps_s[0:1, 0:128], scs[0:1, 3:4])
                dve("tensor_copy", scs[0:1, 3:4], rows_a[0:1, i * 128 + 127: i * 128 + 128])
                vr = dve("reciprocal", rows_a[64:65, i * 128:(i + 1) * 128],
                         rows_a[0:1, i * 128:(i + 1) * 128], sem=DVE)
                P.consumed("ps_s", DVE, vr)
                P.wait("tensor", DVE, vr)
                P.need("tensor", "ps_s")
                vb = mm(ps_s[:, 128:256], onesf[64:65, :],
                        rows_a[64:65, i * 128:(i + 1) * 128], True, True, sem=PE)
                P.wait("vector", PE, vb)
                P.need("vector", "recbc")
                vrb = dve("tensor_copy", recbc[:], ps_s[:, 128:256], sem=DVE)
                P.consumed("ps_s", DVE, vrb)
                P.produced("recbc", DVE, vrb)
                P.waitr("tensor", "exntm")
                P.need("tensor", "ps_num")
                for c in range(8):
                    vn = mm(ps_num[:, c * 128:(c + 1) * 128], exntm[:, c * 128:(c + 1) * 128],
                            Ub[:], True, True, sem=PE if c == 7 else None)
                P.consumed("exntm", PE, vn)
                P.wait("vector", PE, vn)
                P.waitr("vector", "recbc")
                if k == 0:
                    P.need("vector", ("ctxT", pb))
                for c in range(8):
                    dve("scalar_tensor_tensor",
                        ctxg[:, c * 512 + k * 128: c * 512 + (k + 1) * 128],
                        ps_num[:, c * 128:(c + 1) * 128], ncarry[:, c:c + 1], recbc[:],
                        ALU.add, ALU.mult)
                    vc = dve("tensor_tensor", ncarry[:, c:c + 1],
                             ncarry[:, c:c + 1], ps_num[:, c * 128 + 127: c * 128 + 128],
                             ALU.add, sem=DVE if c == 7 else None)
                P.consumed("ps_num", DVE, vc)
                P.consumed("recbc", DVE, vc)
            P.produced(("ctx", g), DVE, vc)

        def tail_group(g):
            pb = g % 2
            ctxg = ctxT[pb]
            P.waitr("tensor", ("ctx", g))
            P.waitr("tensor", "wct")
            vw_last = None
            for mp in range(4):
                va_pair = None
                for mmi in range(2):
                    m = 2 * mp + mmi
                    P.need("tensor", ("ps_wc", mmi))
                    for kk in range(16):
                        if kk < 8:
                            rh = ctxg[:, kk * 512:(kk + 1) * 512]
                        else:
                            rh = xnT[:, (kk - 8) * T + g * 512:(kk - 8) * T + (g + 1) * 512]
                        vw = mm(ps_wc[mmi][:],
                                wct[:, kk * 1024 + m * 128: kk * 1024 + (m + 1) * 128],
                                rh, start=(kk == 0), stop=(kk == 15),
                                sem=PE if kk == 15 else None)
                    vw_last = vw
                    P.wait("scalar", PE, vw)
                    P.need("scalar", ("afm", 0))
                    va_pair = act(afm[:, mmi * 512:(mmi + 1) * 512],
                                  ps_wc[mmi][:], AF.Tanh, bias=bca[:, m:m + 1], sem=ACT)
                    P.consumed(("ps_wc", mmi), ACT, va_pair)
                P.wait("tensor", ACT, va_pair)
                P.need("tensor", "ps_tr")
                for k in range(4):
                    for mmi in range(2):
                        vt = tr(ps_tr[:, (2 * k + mmi) * 128:(2 * k + mmi + 1) * 128],
                                afm[:, mmi * 512 + k * 128: mmi * 512 + (k + 1) * 128],
                                sem=PE if (k == 3 and mmi == 1) else None)
                P.consumed(("afm", 0), PE, vt)
                P.wait("vector", PE, vt)
                dst = atm.ap().rearrange("p (k q f) -> p k q f", k=4, q=8)[:, :, 2 * mp:2 * mp + 2, :]
                src = ps_tr.ap().rearrange("p (k mmi f) -> p k mmi f", k=4, mmi=2)
                ve = dve("tensor_copy", dst, src, sem=DVE)
                P.consumed("ps_tr", DVE, ve)
            P.produced(("atm", g), DVE, ve)
            P.consumed(("ctxT", pb), PE, vw_last)

            for k in range(4):
                ln_stats_tile(atm[:, k * 1024:(k + 1) * 1024], k, ("atm", g))
            ln_rstd(slice(0, 4), D, "ln2")
            for k in range(4):
                i = 4 * g + k
                P.waitr("scalar", ("rstd", "ln2"))
                P.need("scalar", "tmp32")
                va = act(tmp32[:], atm[:, k * 1024:(k + 1) * 1024], AF.Identity,
                         bias=nbias[:, k:k + 1], scale=rstd[:, k:k + 1], sem=ACT)
                P.wait("vector", ACT, va)
                vy = dve("tensor_tensor", stream[:, i * 1024:(i + 1) * 1024],
                         stream[:, i * 1024:(i + 1) * 1024], tmp32[:], ALU.add, sem=DVE)
                P.consumed("tmp32", DVE, vy)
                P.produced(("y1", b, i), DVE, vy)
            for k in range(4):
                i = 4 * g + k
                ln_stats_tile(stream[:, i * 1024:(i + 1) * 1024], 4 + k, ("y1", b, i))
            ln_rstd(slice(4, 8), D, "ln3", out_r=rstd3, out_b=nbias3,
                    ocols=slice(4 * g, 4 * g + 4))
            for k in range(4):
                i = 4 * g + k
                buf = k % 2
                xt = xn_tmp[:, buf * 1024:(buf + 1) * 1024]
                P.waitr("scalar", ("rstd", "ln3"))
                P.need("scalar", ("xn_tmp", buf))
                va = act(xt, stream[:, i * 1024:(i + 1) * 1024], AF.Identity,
                         bias=nbias3[:, i:i + 1], scale=rstd3[:, i:i + 1], sem=ACT)
                P.wait("sync", ACT, va)
                vs2 = dma("sync", xn2_d[(b * T + i * 128):(b * T + (i + 1) * 128), :], xt,
                          XT[buf])
                P.consumed(("xn_tmp", buf), XT[buf], vs2)

            vg = vs1 = None
            for i in range(4 * g, 4 * g + 4):
                P.waitr("scalar", ("rstd", "ln3"))
                P.need("scalar", "tmp32")
                vaf = act(tmp32[:], stream[:, i * 1024:(i + 1) * 1024], AF.Identity,
                          bias=nbias3[:, i:i + 1], scale=rstd3[:, i:i + 1], sem=ACT)
                P.wait("tensor", ACT, vaf)
                P.need("tensor", ("ps_h", 0))
                P.need("tensor", ("ps_h", 1))
                for c in range(8):
                    vt = tr(ps_h[c // 4][:, (c % 4) * 128:(c % 4 + 1) * 128],
                            tmp32[:, c * 128:(c + 1) * 128],
                            fp32=True, sem=PE if c == 7 else None)
                P.consumed("tmp32", PE, vt)
                P.wait("vector", PE, vt)
                P.need("vector", "xn2fT")
                dve("tensor_copy", xn2fT[:, 0:512], ps_h[0][:])
                vef = dve("tensor_copy", xn2fT[:, 512:1024], ps_h[1][:], sem=DVE)
                P.consumed(("ps_h", 0), DVE, vef)
                P.consumed(("ps_h", 1), DVE, vef)
                P.produced("xn2fT", DVE, vef)
                P.waitr("tensor", "xn2fT")
                P.need("tensor", ("ps_wc", 0))
                for c in range(8):
                    vl = mm(ps_wc[0][:, 0:8],
                            xn2fT[:, c * 128:(c + 1) * 128],
                            rwcf[:, c * 8:(c + 1) * 8], start=(c == 0), stop=(c == 7),
                            sem=PE if c == 7 else None)
                P.consumed("xn2fT", PE, vl)
                P.wait("vector", PE, vl)
                lg = lgscr[:, 0:8]
                dve("tensor_tensor", lg, ps_wc[0][:, 0:8], rbc[:], ALU.add)
                vq = dve("tensor_reduce", lgscr[:, 8:9], lg, sem=DVE, axis=AX.X, op=ALU.max)
                P.consumed(("ps_wc", 0), DVE, vq)
                dve("tensor_single_scalar", lgscr[:, 8:16], lg, lgscr[:, 8:9], ALU.is_equal)
                dve("tensor_tensor", lgscr[:, 16:24], lgscr[:, 8:16], iota8[:], ALU.mult)
                dve("tensor_scalar", lgscr[:, 24:32], lgscr[:, 8:16], -1000.0, 1000.0,
                    ALU.mult, ALU.add)
                dve("tensor_tensor", lgscr[:, 16:24], lgscr[:, 16:24], lgscr[:, 24:32], ALU.add)
                col = 16 * b + i
                dve("tensor_reduce", t1f[:, col:col + 1], lgscr[:, 16:24], axis=AX.X, op=ALU.min)
                dve("tensor_copy", t1cols[:, col:col + 1], t1f[:, col:col + 1])
                dve("tensor_reduce", lgscr[:, 24:25], lg, axis=AX.X, op=ALU.max)
                vneg = dve("tensor_scalar_mul", lgscr[:, 24:25], lgscr[:, 24:25], -1.0, sem=DVE)
                P.wait("scalar", DVE, vneg)
                va = act(lgscr[:, 32:40], lg, AF.Exp, bias=lgscr[:, 24:25], sem=ACT)
                P.wait("vector", ACT, va)
                dve("tensor_reduce", lgscr[:, 25:26], lgscr[:, 32:40], axis=AX.X, op=ALU.add)
                vg = dve("reciprocal", gcol[:, col:col + 1], lgscr[:, 25:26], sem=DVE)
                P.waitr("sync", ("y1", b, i))
                vs1 = dma("sync", y1_d[(b * T + i * 128):(b * T + (i + 1) * 128), :],
                          stream[:, i * 1024:(i + 1) * 1024], Y1S)
            return vg, vs1

        for g in range(NG):
            cumsum_group(g)
            if g >= 1:
                vg, vs1 = tail_group(g - 1)
        vg, vs1 = tail_group(NG - 1)
        P.produced("gates", DVE, vg)
        P.produced(("y1stored", b), Y1S, vs1)
        P.wait("sync", DVE, vg)
        for i in range(NT):
            dma("sync", g_d[(b * T + i * 128):(b * T + (i + 1) * 128), :],
                gcol[:, 16 * b + i:16 * b + i + 1], GD)

    def phase_ffn(SB):
        wbuf, y_sb, xg_tm, xgT, z_sb, bias_sb, idxf, idxi, idxrow, y1t = (
            SB["wbuf"], SB["y_sb"], SB["xg_tm"], SB["xgT"], SB["z"],
            SB["bias"], SB["idxf"], SB["idxi"], SB["idxrow"], SB["y1t"])
        ysc = [SB["ysc"][0], SB["ysc"][1], SB["moet"][0], SB["moet"][1]]

        v_xt = (XT[0].n, XT[1].n, Y1S.n)
        P.wait("gpsimd", XT[0], v_xt[0])
        P.wait("gpsimd", XT[1], v_xt[1])
        P.wait("gpsimd", Y1S, v_xt[2])
        P.wait("gpsimd", GD, GD.n)

        voh = None
        for t in range(2 * NT):
            voh = dve("tensor_single_scalar", oh_all[:, t * 8:(t + 1) * 8], iota8[:],
                      t1f[:, t:t + 1], ALU.is_equal,
                      sem=DVE if t == 2 * NT - 1 else None)
        vrank = None
        for g in range(8):
            hp = g % 2
            P.need("tensor", ("ps_h", hp))
            P.wait("tensor", DVE, voh)
            vm = None
            for q in range(4):
                t = 4 * g + q
                vm = mm(ps_h[hp][:, q * 8:(q + 1) * 8], Uf[:],
                        oh_all[:, t * 8:(t + 1) * 8], True, True,
                        sem=PE if q == 3 else None)
            P.wait("vector", PE, vm)
            P.need("vector", "cntsb")
            vq = dve("tensor_copy", cntsb[:], ps_h[hp][:, 0:32], sem=DVE)
            P.consumed(("ps_h", hp), DVE, vq)
            P.wait("sync", DVE, vq)
            vd = None
            for q in range(4):
                t = 4 * g + q
                vd = dma("sync", c8T[t:t + 1, :], cntsb[127:128, q * 8:(q + 1) * 8],
                         C8S)
            P.consumed("cntsb", C8S, vd)
            vmu = dve("tensor_tensor", rkm[:], cntsb[:],
                      oh_all[:, g * 32:(g + 1) * 32], ALU.mult, sem=DVE)
            P.wait("scalar", DVE, vmu)
            for q in range(4):
                t = 4 * g + q
                vrank = act(sqscr[:, 0:8], rkm[:, q * 8:(q + 1) * 8], AF.Identity,
                            accum=rank_all[:, t:t + 1], sem=ACT)
        P.wait("tensor", C8S, C8S.n)
        P.need("tensor", "ps_s")
        vm = mm(ps_s[0:8, 0:1], c8T[:], onesf[0:32, 0:1], True, True, sem=PE)
        P.wait("vector", PE, vm)
        vq = dve("tensor_copy", m8[0:8, 0:1], ps_s[0:8, 0:1], sem=DVE)
        P.consumed("ps_s", DVE, vq)
        dve("tensor_scalar", m8[0:8, 1:2], m8[0:8, 0:1], float(CH - 1), 1.0 / CH,
            ALU.add, ALU.mult)
        dve("tensor_copy", SB["capi"][0:8, 0:1], m8[0:8, 1:2])
        vq = dve("tensor_copy", m8[0:8, 1:2], SB["capi"][0:8, 0:1], sem=DVE)
        P.wait("tensor", DVE, vq)
        P.need("tensor", "ps_s")
        vm = mm(ps_s[0:1, 0:8], m8[0:8, 1:2], Uf[0:8, 0:8], True, True, sem=PE)
        P.wait("vector", PE, vm)
        vq = dve("tensor_copy", rows_a[0:1, 0:8], ps_s[0:1, 0:8], sem=DVE)
        P.consumed("ps_s", DVE, vq)
        P.wait("tensor", DVE, vq)
        P.need("tensor", "ps_s")
        vm = mm(ps_s[0:8, 8:9], rows_a[0:1, 0:8], onesf[0:1, 0:1], True, True, sem=PE)
        P.wait("vector", PE, vm)
        vq = dve("tensor_copy", m8[0:8, 2:3], ps_s[0:8, 8:9], sem=DVE)
        P.consumed("ps_s", DVE, vq)
        dve("tensor_tensor", m8[0:8, 3:4], m8[0:8, 2:3], m8[0:8, 1:2], ALU.subtract)
        dve("tensor_scalar_mul", m8[0:8, 3:4], m8[0:8, 3:4], float(CH))
        P.need("tensor", "ps_s")
        vm = mm(ps_s[0:8, 32:64], c8T[:], u32s[:], True, True, sem=PE)
        P.wait("vector", PE, vm)
        vb = dve("tensor_scalar", base32[:], ps_s[0:8, 32:64], m8[0:8, 3:4], -1.0,
                 ALU.add, ALU.add, sem=DVE)
        P.consumed("ps_s", DVE, vb)
        P.wait("sync", DVE, vb)
        vbf = dma("sync", baseflat.rearrange("(e t) one -> e (t one)", e=8),
                  base32[:], BFD)
        dve("scalar_tensor_tensor", idxgf[:], t1f[:], 32.0, tcol[:],
            ALU.mult, ALU.add)
        vi = dve("tensor_copy", idxgi[:], idxgf[:], sem=DVE)
        P.wait("gpsimd", DVE, vi)
        P.wait("gpsimd", BFD, vbf)
        for t in range(2 * NT):
            idma(bsel_all[:, t:t + 1], None, baseflat,
                 IndirectOffsetOnAxis(idxgi[:, t:t + 1], 0), 255, BSG)
        vq = dve("tensor_single_scalar", SB["cmp16"][0:8, :], iota16[:], m8[0:8, 2:3],
                 ALU.is_ge, sem=DVE)
        P.wait("tensor", DVE, vq)
        P.need("tensor", "ps_s")
        vm = mm(ps_s[0:1, 16:32], onesf[0:8, 0:1], SB["cmp16"][0:8, :], True, True, sem=PE)
        P.wait("vector", PE, vm)
        vq = dve("tensor_single_scalar", rows_a[0:1, 16:32], ps_s[0:1, 16:32], 7.0,
                 ALU.min, sem=DVE)
        P.consumed("ps_s", DVE, vq)
        P.wait("tensor", DVE, vq)
        P.need("tensor", "ps_s")
        vm = mm(ps_s[:, 32:48], onesf[0:1, :], rows_a[0:1, 16:32], True, True, sem=PE)
        P.wait("vector", PE, vm)
        vq = dve("tensor_copy", echunk_bc[:], ps_s[:, 32:48], sem=DVE)
        P.consumed("ps_s", DVE, vq)
        P.produced("echunk", DVE, vq)


        P.waitr("vector", "echunk")
        dve("tensor_scalar_mul", lgscr[:, 0:16], echunk_bc[:], 1024.0)
        for j in range(NCHUNK):
            dve("tensor_scalar_add", SB["widxf"][:, j * 8:(j + 1) * 8],
                fbpat[:, j * 8:(j + 1) * 8], lgscr[:, j:j + 1])
        vq = dve("tensor_copy", SB["widxi"][:], SB["widxf"][:], sem=DVE)
        P.produced("widxall", DVE, vq)
        for j in range(NCHUNK):
            dve("scalar_tensor_tensor", SB["bidxf"][:, j:j + 1], echunk_bc[:, j:j + 1],
                128.0, iotap[:], ALU.mult, ALU.add)
        vq = dve("tensor_copy", SB["bidxi"][:], SB["bidxf"][:], sem=DVE)
        P.produced("bidxall", DVE, vq)

        P.wait("vector", BSG, BSG.n)
        P.wait("vector", ACT, vrank)
        dve("tensor_tensor", dest32f[:], bsel_all[:], rank_all[:], ALU.add)
        vq = dve("tensor_copy", dest32i[:], dest32f[:], sem=DVE)
        P.wait("gpsimd", DVE, vq)
        vsc = None
        for t in range(2 * NT):
            vsc = idma(inv_d, IndirectOffsetOnAxis(dest32i[:, t:t + 1], 0),
                       tokall[:, t:t + 1], None, NSLOT - 1, GS)
        P.produced("invscat", GS, vsc)

        stagecut(5, [("dbg_g", gcol[:]), ("dbg_t1", t1cols[:]),
                     ("dbg_inv", inv_d), ("dbg_m8", m8[:]),
                     ("dbg_xn2", xn2_d)])

        inv_row = inv_d.rearrange("s one -> one s")
        out_flat = out_h.rearrange("b t d -> (b t) d")
        GG = GS2
        GY = MT
        OS = [SO, MT2]

        def load_chunk_indices(j, par):
            P.need("gpsimd", ("idxrow", par))
            P.waitr("gpsimd", "invscat")
            vr = dma("gpsimd", idxrow[par][:], inv_row[0:1, j * CH:(j + 1) * CH], GI[par])
            P.produced(("idxrowld", par), GI[par], vr)
            P.waitr("tensor", ("idxrowld", par))
            P.need("tensor", "ps_s")
            for s in range(4):
                vm = mm(ps_s[:, 448 + s:449 + s], idxrow[par][0:1, s * 128:(s + 1) * 128],
                        onesf[0:1, 0:1], True, True, sem=PE if s == 3 else None)
            P.wait("vector", PE, vm)
            for s in range(4):
                P.need("vector", ("idxi", par, s))
                P.need("vector", ("idxiY", par, s))
            P.need("vector", ("idxiG", par))
            dve("tensor_copy", idxf[:, par * 4:par * 4 + 4], ps_s[:, 448:452])
            vq = dve("tensor_copy", idxi[:, par * 4:par * 4 + 4],
                     idxf[:, par * 4:par * 4 + 4], sem=DVE)
            P.consumed("ps_s", DVE, vq)
            P.consumed(("idxrow", par), DVE, vq)
            P.produced(("idxiok", par), DVE, vq)

        def gather_chunk(j, par):
            ve = None
            for s in range(4):
                gb = s % 2
                P.need("gpsimd", ("xg_tm", gb))
                P.waitr("gpsimd", ("idxiok", par))
                vg = idma(xg_tm[gb][:], None, xn2_d,
                          IndirectOffsetOnAxis(idxi[:, par * 4 + s:par * 4 + s + 1], 0),
                          NTOK - 1, GX[gb])
                P.produced(("xg", gb), GX[gb], vg)
                P.waitr("tensor", ("xg", gb))
                P.need("tensor", "ps_tr")
                for c in range(8):
                    vt = tr(ps_tr[:, c * 128:(c + 1) * 128],
                            xg_tm[gb][:, c * 128:(c + 1) * 128],
                            sem=PE if c == 7 else None)
                P.consumed(("xg_tm", gb), PE, vt)
                P.wait("vector", PE, vt)
                P.need("vector", ("xgT", par))
                dst = xgT[par].ap().rearrange("p (c t) -> p c t", c=8)[:, :, s * 128:(s + 1) * 128]
                src = ps_tr.ap().rearrange("p (c t) -> p c t", c=8)
                ve = dve("tensor_copy", dst, src, sem=DVE)
                P.consumed("ps_tr", DVE, ve)
            P.produced(("xgTok", par), DVE, ve)

        def load_bias(j, par):
            P.need("gpsimd", ("bias", par))
            P.need("gpsimd", ("biasd", par))
            P.waitr("gpsimd", "bidxall")
            vb = idma(bias_sb[par][:], None, fbias_flat,
                      IndirectOffsetOnAxis(SB["bidxi"][:, j:j + 1], 0),
                      NE * 128 - 1, GB[par])
            P.produced(("biasok", par), GB[par], vb)

        def wload(k):
            slot = k % 4
            P.need("gpsimd", ("wbuf", slot))
            P.waitr("gpsimd", "widxall")
            vw = idma(wbuf[slot][:], None, fw_flat,
                      IndirectOffsetOnAxis(SB["widxi"][:, k:k + 1], 0),
                      NE * NFB * 128 - 1, LW[slot])
            P.produced(("w", slot), LW[slot], vw)

        def h_phase(j, fb, wb, zb, par):
            P.waitr("tensor", ("w", wb))
            P.waitr("tensor", ("xgTok", par))
            P.waitr("scalar", ("biasok", par))
            vh = None
            for fc in range(4):
                hp = fc % 2
                P.need("tensor", ("ps_h", hp))
                for c in range(8):
                    vh = mm(ps_h[hp][:],
                            wbuf[wb][:, c * 512 + fc * 128: c * 512 + (fc + 1) * 128],
                            xgT[par][:, c * 512:(c + 1) * 512],
                            start=(c == 0), stop=(c == 7),
                            sem=PE if c == 7 else None)
                P.wait("scalar", PE, vh)
                P.need("scalar", ("z", zb, fc))
                va = act(z_sb[zb][:, fc * 512:(fc + 1) * 512], ps_h[hp][:],
                         AF.Relu, bias=bias_sb[par][:, fb * 4 + fc: fb * 4 + fc + 1],
                         sem=ACT)
                P.consumed(("ps_h", hp), ACT, va)
                P.produced(("zr", zb, fc), ACT, va)
            return vh

        def y_phase(j, fb, wb, zb, par):
            for fc in range(4):
                P.waitr("tensor", ("zr", zb, fc))
            P.waitr("vector", ("biasok", par))
            vy = None
            vv = None
            for dc in range(8):
                pb = dc % 4
                P.need("tensor", ("ps_y", pb))
                for jj in range(4):
                    vy = mm(ybank[pb],
                            wbuf[wb][:, 4096 + jj * 1024 + dc * 128:
                                     4096 + jj * 1024 + (dc + 1) * 128],
                            z_sb[zb][:, jj * 512:(jj + 1) * 512],
                            start=(jj == 0), stop=(jj == 3),
                            sem=PE if jj == 3 else None)
                P.wait("vector", PE, vy)
                dstp = y_sb[par][:, dc * 512:(dc + 1) * 512]
                if fb == 0:
                    P.need("vector", ("y", par))
                    P.need("vector", ("yB", par))
                    vv = dve("tensor_scalar_add", dstp, ybank[pb],
                             bias_sb[par][:, 32 + dc:33 + dc], sem=DVE)
                else:
                    vv = dve("tensor_tensor", dstp, dstp, ybank[pb], ALU.add, sem=DVE)
                P.consumed(("ps_y", pb), DVE, vv)
            for fc in range(4):
                P.consumed(("z", zb, fc), PE, vy)
            P.consumed(("wbuf", wb), PE, vy)
            P.produced(("yok", par), DVE, vv)
            if fb == 0:
                P.consumed(("biasd", par), DVE, vv)
            return vy

        def scatter_back(j, par):
            P.waitr("tensor", ("yok", par))
            vt3 = None
            va_sq = None
            vgg = None
            for s in range(4):
                P.need("tensor", "ps_num")
                P.need("tensor", ("ps_y", 2))
                P.need("tensor", ("ps_y", 3))
                for dc in range(8):
                    vt = tr(ps_num[:, dc * 128:(dc + 1) * 128],
                            y_sb[par][:, dc * 512 + s * 128: dc * 512 + (s + 1) * 128],
                            fp32=True, sem=PE if dc == 7 else None)
                vt3 = vt
                P.wait("vector", PE, vt)
                ve = dve("tensor_copy", ysc[s][:], ps_num[:, 0:1024], sem=DVE)
                P.consumed("ps_num", DVE, ve)
                P.consumed(("ps_y", 2), DVE, ve)
                P.consumed(("ps_y", 3), DVE, ve)
                dve("tensor_reduce", sums[:, s:s + 1], ysc[s][:], axis=AX.X, op=ALU.add)
                P.wait("scalar", DVE, ve)
                va_sq = act(sqscr[:], ysc[s][:], AF.Square, accum=ssq[:, s:s + 1],
                            sem=ACT)
                P.waitr("gpsimd", ("idxiok", par))
                if s == 0:
                    P.need("gpsimd", ("gbuf", par))
                vgg = idma(gbuf[:, par * 4 + s:par * 4 + s + 1], None, g_d,
                           IndirectOffsetOnAxis(idxi[:, par * 4 + s:par * 4 + s + 1], 0),
                           NTOK - 1, GG[par])
                P.need("gpsimd", ("y1t", s))
                vy1 = idma(y1t[s][:], None, y1_d,
                           IndirectOffsetOnAxis(idxi[:, par * 4 + s:par * 4 + s + 1], 0),
                           NTOK - 1, GY[s])
                P.produced(("y1g", s), GY[s], vy1)
                P.consumed(("idxiY", par, s), GY[s], vy1)
            P.consumed(("y", par), PE, vt3)
            P.consumed(("yB", par), PE, vt3)
            P.consumed(("idxiG", par), GG[par], vgg)
            P.wait("vector", ACT, va_sq)
            P.wait("vector", GG[par], vgg)
            dve("tensor_scalar_mul", mcol[:, 0:4], sums[:, 0:4], 1.0 / D)
            dve("tensor_scalar_mul", varc[:, 0:4], ssq[:, 0:4], 1.0 / D)
            dve("tensor_tensor", lgscr[:, 40:44], mcol[:, 0:4], mcol[:, 0:4], ALU.mult)
            dve("tensor_tensor", varc[:, 0:4], varc[:, 0:4], lgscr[:, 40:44],
                ALU.subtract)
            gq = gbuf[:, par * 4:par * 4 + 4]
            dve("tensor_tensor", SB["g2"][:], gq, gq, ALU.mult)
            vg2 = dve("tensor_tensor", varc[:, 0:4], varc[:, 0:4], SB["g2"][:],
                      ALU.mult, sem=DVE)
            P.consumed(("gbuf", par), DVE, vg2)
            v = dve("tensor_scalar_add", varc[:, 0:4], varc[:, 0:4], EPS, sem=DVE)
            P.wait("scalar", DVE, v)
            va = act(lgscr[:, 48:52], varc[:, 0:4], AF.Sqrt, sem=ACT)
            P.wait("vector", ACT, va)
            dve("reciprocal", rstd[:, 0:4], lgscr[:, 48:52])
            dve("tensor_tensor", SB["grstd"][:], gq, rstd[:, 0:4], ALU.mult)
            dve("tensor_tensor", nbias[:, 0:4], mcol[:, 0:4], SB["grstd"][:], ALU.mult)
            dve("tensor_scalar_mul", nbias[:, 0:4], nbias[:, 0:4], -1.0)
            for s in range(4):
                dve("tensor_scalar", ysc[s][:], ysc[s][:], SB["grstd"][:, s:s + 1],
                    nbias[:, s:s + 1], ALU.mult, ALU.add)
                P.waitr("vector", ("y1g", s))
                vadd = dve("tensor_tensor", y1t[s][:], y1t[s][:], ysc[s][:], ALU.add,
                           sem=DVE)
                P.wait("gpsimd", DVE, vadd)
                vout = idma(out_flat,
                            IndirectOffsetOnAxis(idxi[:, par * 4 + s:par * 4 + s + 1], 0),
                            y1t[s][:], None, NTOK - 1, OS[par][s])
                P.consumed(("y1t", s), OS[par][s], vout)
                P.consumed(("idxi", par, s), OS[par][s], vout)

        ybank = [ps_wc[0].ap()[:], ps_wc[1].ap()[:],
                 ps_num.ap()[:, 0:512], ps_num.ap()[:, 512:1024]]

        NPAIR = NCHUNK * NFB
        load_chunk_indices(0, 0)
        gather_chunk(0, 0)
        load_bias(0, 0)
        for kk in range(min(3, NPAIR)):
            wload(kk)
        prev = None
        for k in range(NPAIR):
            j, fb = divmod(k, NFB)
            par = j % 2
            if fb == 4 and j + 1 < NCHUNK:
                load_chunk_indices(j + 1, (j + 1) % 2)
                gather_chunk(j + 1, (j + 1) % 2)
                load_bias(j + 1, (j + 1) % 2)
            vh_last = h_phase(j, fb, k % 4, k % 2, par)
            if fb == NFB - 1:
                P.consumed(("xgT", par), PE, vh_last)
                P.consumed(("bias", par), ACT, ACT.n)
            if prev is not None:
                jp, fbp = divmod(prev, NFB)
                y_phase(jp, fbp, prev % 4, prev % 2, jp % 2)
                if fbp == NFB - 1:
                    scatter_back(jp, jp % 2)
            if k + 3 < NPAIR:
                wload(k + 3)
            prev = k
        jp, fbp = divmod(prev, NFB)
        y_phase(jp, fbp, prev % 4, prev % 2, jp % 2)
        scatter_back(jp, jp % 2)

        stagecut(6, [("dbg_moe", moe_d), ("dbg_inv", inv_d), ("dbg_xn2", xn2_d),
                     ("dbg_g", gcol[:]), ("dbg_t1", t1cols[:])])

        for par in range(2):
            for s in range(4):
                P.wait("sync", OS[par][s], OS[par][s].n)

    def fence():
        for eng in ENGS:
            P.wait(eng, PE, PE.n)
            P.wait(eng, ACT, ACT.n)
            P.wait(eng, DVE, DVE.n)

    open_stacks = []
    try:
        esa = ExitStack()
        open_stacks.append(esa)
        sba = lambda n, s, dt: esa.enter_context(nc.sbuf_tensor(n, s, dt))
        SA = {
            "stream": sba("stream", [128, NT * 1024], F32),
            "xnT": sba("xnT", [128, 8 * T], BF16),
            "w1t": sba("w1t", [128, 8192], BF16),
            "wct": sba("wct", [128, 16384], BF16),
        }
        P.wait("gpsimd", LWA, LWA.n)
        vw1 = dma("gpsimd", SA["w1t"][:], h_in["w1t_p"], LWA)
        vwc = dma("gpsimd", SA["wct"][:], h_in["wct_p"], LWA)
        P.produced("w1t", LWA, vwc)
        P.produced("wct", LWA, vwc)
        for b in range(BPC):
            esb = ExitStack()
            open_stacks.append(esb)
            sbb = lambda n, s, dt: esb.enter_context(nc.sbuf_tensor(f"{n}_{b}", s, dt))
            AB = {
                "xn_tmp": sbb("xn_tmp", [128, 2048], BF16),
                "exnfm": sbb("exnfm", [128, 1024], BF16),
                "exntm": sbb("exntm", [128, 1024], BF16),
                "hsb": sbb("hsb", [128, 1024], BF16),
                "afm": sbb("afm", [128, 1024], BF16),
                "atm": sbb("atm", [128, 4096], BF16),
                "ebc": sbb("ebc", [128, 512], BF16),
                "ctxT": [sbb("ctxT0", [128, 4096], BF16),
                         sbb("ctxT1", [128, 4096], BF16)],
            }
            fence()
            phase_a(b, AB, SA)
            esb.close()
            open_stacks.remove(esb)
        esa.close()
        open_stacks.remove(esa)

        esf = ExitStack()
        open_stacks.append(esf)
        sbf = lambda n, s, dt: esf.enter_context(nc.sbuf_tensor(n, s, dt))
        SB = {
            "wbuf": [sbf(f"wbuf{i}", [128, 8192], BF16) for i in range(4)],
            "y_sb": [sbf("ysb0", [128, 4096], F32), sbf("ysb1", [128, 4096], F32)],
            "xg_tm": [sbf("xgtm0", [128, 1024], BF16), sbf("xgtm1", [128, 1024], BF16)],
            "xgT": [sbf("xgT0", [128, 4096], BF16), sbf("xgT1", [128, 4096], BF16)],
            "ysc": [sbf("ysc0", [128, 1024], F32), sbf("ysc1", [128, 1024], F32)],
            "z": [sbf("z0", [128, 2048], BF16), sbf("z1", [128, 2048], BF16)],
            "bias": [sbf("bias0", [128, 40], F32), sbf("bias1", [128, 40], F32)],
            "idxf": sbf("idxf", [128, 8], F32),
            "idxi": sbf("idxi", [128, 8], I32),
            "idxrow": [sbf("idxrow0", [1, 512], F32), sbf("idxrow1", [1, 512], F32)],
            "moet": [sbf(f"moet{i}", [128, 1024], F32) for i in range(4)],
            "y1t": [sbf(f"y1t{i}", [128, 1024], F32) for i in range(4)],
            "destf": sbf("destf", [128, 1], F32),
            "desti": sbf("desti", [128, 1], I32),
            "tokf": sbf("tokf", [128, 1], F32),
            "capi": sbf("capi", [128, 1], I32),
            "cmp16": sbf("cmp16", [8, 16], F32),
            "widxf": sbf("widxf", [128, 128], F32),
            "widxi": sbf("widxi", [128, 128], I32),
            "bidxf": sbf("bidxf", [128, 16], F32),
            "bidxi": sbf("bidxi", [128, 16], I32),
            "g2": sbf("g2", [128, 4], F32),
            "grstd": sbf("grstd", [128, 4], F32),
        }
        fence()
        phase_ffn(SB)
        esf.close()
        open_stacks.remove(esf)
    except StageStop:
        for st_ in reversed(open_stacks):
            st_.close()

    P.wait("sync", ST, ST.n)
    for _so in SO:
        P.wait("sync", _so, _so.n)

    with nc.Block() as block:
        def run(eng_items):
            def f(E):
                for fn in eng_items:
                    fn(E)
            return f
        block.sync(run(P.items["sync"]))
        block.tensor(run(P.items["tensor"]))
        block.vector(run(P.items["vector"]))
        block.scalar(run(P.items["scalar"]))
        block.gpsimd(run(P.items["gpsimd"]))

    es.close()
    return nc


_NC_CACHE = {}


def kernel(**inputs):
    if "nc" not in _NC_CACHE:
        _NC_CACHE["nc"] = build()
    nc = _NC_CACHE["nc"]
    packed = pack_host(inputs)
    x = np.asarray(inputs["L0_attn_x"], np.float32)
    in_maps = []
    for c in range(NCORES):
        m = dict(packed)
        m["x"] = np.ascontiguousarray(x[2 * c:2 * c + 2])
        in_maps.append(m)
    res = run_bass_kernel_spmd(nc, in_maps, core_ids=list(range(NCORES)))
    return np.concatenate([r["out"] for r in res.results], axis=0)
